# version 01 (final)
# baseline (speedup 1.0000x reference)
import numpy as np
import concourse.bass as bass
import concourse.mybir as mybir
from concourse.bass_utils import run_bass_kernel_spmd

F32 = mybir.dt.float32
BF16 = mybir.dt.bfloat16
AF = mybir.ActivationFunctionType
ALU = mybir.AluOpType
AX = mybir.AxisListType

T = 4096
D = 1024
NCORES = 8
NT = T // 128

ENGINES = ('tensor', 'vector', 'scalar', 'gpsimd', 'sync')


class Ser:
    def __init__(self, nc, sem):
        self.nc = nc
        self.sem = sem
        self.ops = []
        self.cnt = 0
    def emit(self, engine, fn, inc):
        self.ops.append((engine, fn, self.cnt, inc))
        self.cnt += inc
    def t(self, fn): self.emit('tensor', fn, 1)
    def v(self, fn): self.emit('vector', fn, 1)
    def s(self, fn): self.emit('scalar', fn, 1)
    def dma(self, fn): self.emit('sync', fn, 16)
    def build(self):
        nc, sem = self.nc, self.sem
        final = self.cnt
        by_eng = {e: [] for e in ENGINES}
        for (e, fn, cb, inc) in self.ops:
            by_eng[e].append((fn, cb, inc))
        def make_section(eng, eng_name):
            for (fn, cb, inc) in by_eng[eng_name]:
                if cb > 0:
                    eng.wait_ge(sem, cb)
                inst = fn()
                inst.then_inc(sem, inc)
            if eng_name == 'gpsimd':
                eng.wait_ge(sem, final)
        with nc.Block() as block:
            block.tensor(lambda eng: make_section(eng, 'tensor'))
            block.vector(lambda eng: make_section(eng, 'vector'))
            block.scalar(lambda eng: make_section(eng, 'scalar'))
            block.gpsimd(lambda eng: make_section(eng, 'gpsimd'))
            block.sync(lambda eng: make_section(eng, 'sync'))


class Blk:
    def __init__(self, name, pool_a_len=8 * T, pool_w_len=8 * 1024, x_shape=(T, D), bfa=True):
        self.nc = bass.Bass("TRN2", target_bir_lowering=False)
        self.name = name
        self.stack = []
        self.consts = {}
        self._uidc = 0
        nc = self.nc
        self.x = nc.dram_tensor("x_in", list(x_shape), F32, kind="ExternalInput").ap()
        self.y = nc.dram_tensor("y_out", [T, D], F32, kind="ExternalOutput").ap()
        self.ident_d = self.cin("c_ident", np.eye(128, dtype=np.float32))
        self.ones1_d = self.cin("c_ones1", np.ones((1, 512), dtype=np.float32))
        self.pool_a_len = pool_a_len
        self.pool_w_len = pool_w_len
        self._bfa = bfa

    def uid(self):
        self._uidc += 1
        return str(self._uidc)

    def cin(self, name, arr):
        arr = np.ascontiguousarray(arr.astype(np.float32))
        self.consts[name] = arr
        return self.nc.dram_tensor(name, list(arr.shape), F32, kind="ExternalInput").ap()

    def win(self, name, shape):
        return self.nc.dram_tensor(name, list(shape), F32, kind="ExternalInput").ap()

    def scratch(self, name, shape, dtype=F32):
        return self.nc.dram_tensor(name, list(shape), dtype, kind="Internal").ap()

    def sb(self, name, shape, dtype=F32):
        h = self.nc.sbuf_tensor(name, list(shape), dtype)
        t = h.__enter__()
        self.stack.append(h)
        return t[:]

    def begin(self):
        h = self.nc.semaphore("gsem")
        sem = h.__enter__()
        self.stack.append(h)
        self.S = Ser(self.nc, sem)
        nc = self.nc
        self.ident = self.sb("s_ident", [128, 128])
        self.ones1 = self.sb("s_ones1", [1, 512])
        self.pool_a = self.sb("s_pool_a", [128, self.pool_a_len], BF16)
        self.pool_w = self.sb("s_pool_w", [128, self.pool_w_len], BF16)
        self.tmp1 = self.sb("s_tmp1", [128, 4096])
        self.tmp2 = self.sb("s_tmp2", [128, 4096])
        if self._bfa:
            self.bfa = self.sb("s_bfa", [128, 4096], BF16)
        self.stat = self.sb("s_stat", [128, 64])
        ph = self.nc.psum_tensor("s_psum", [128, 4096], F32)
        self.psum = ph.__enter__()[:]
        self.stack.append(ph)
        S = self.S
        S.dma(lambda: nc.sync.dma_start(out=self.ident, in_=self.ident_d))
        S.dma(lambda: nc.sync.dma_start(out=self.ones1, in_=self.ones1_d))
        return S

    def finish(self):
        self.S.build()
        for h in reversed(self.stack):
            h.__exit__(None, None, None)

    def bcast_row(self, src_row_sb, out_sb, n):
        nc, S = self.nc, self.S
        for f0 in range(0, n, 512):
            f1 = min(f0 + 512, n)
            S.t(lambda f0=f0, f1=f1: nc.tensor.matmul(
                self.psum[:, f0:f1], self.ones1[:, 0:128], src_row_sb[:, f0:f1],
                start=True, stop=True))
        S.v(lambda n=n: nc.vector.tensor_copy(out=out_sb[:, 0:n], in_=self.psum[:, 0:n]))

    def load_bias_cols(self, b_dram, out_sb, K):
        nc, S = self.nc, self.S
        for c in range(K // 128):
            S.dma(lambda c=c: nc.sync.dma_start(
                out=out_sb[:, c:c + 1], in_=b_dram[c * 128:(c + 1) * 128].unsqueeze(1)))

    def ln_pass(self, x_dram, outT_drams, eps=1e-5, gb=None, xn32_dram=None,
                double_ln=False, eps2=1e-5):
        nc, S = self.nc, self.S
        xt = self.tmp1[:, 0:D]
        sq = self.tmp1[:, D:2 * D]
        tt = self.tmp1[:, 2 * D:2 * D + 128]
        mu = self.stat[:, 0:1]
        var = self.stat[:, 1:2]
        rstd = self.stat[:, 2:3]
        gsb = bsb = None
        if gb is not None:
            g_d, b_d = gb
            gsb = self.sb(self.name + "_lng" + self.uid(), [128, 8])
            bsb = self.sb(self.name + "_lnb" + self.uid(), [128, 8])
            self.load_bias_cols(g_d, gsb, D)
            self.load_bias_cols(b_d, bsb, D)
        epsb = self.stat[:, 3:4]
        epsb2 = self.stat[:, 4:5]
        S.v(lambda: nc.vector.memset(epsb, eps))
        S.v(lambda: nc.vector.memset(epsb2, eps2))
        def one_ln(e):
            eb = epsb if e == eps else epsb2
            S.v(lambda: nc.vector.reduce_sum(mu, xt, axis=AX.X))
            S.s(lambda: nc.scalar.mul(mu, mu, 1.0 / D))
            S.v(lambda: nc.vector.tensor_scalar(out=xt, in0=xt, scalar1=mu, scalar2=None, op0=ALU.subtract))
            S.v(lambda: nc.vector.tensor_tensor(out=sq, in0=xt, in1=xt, op=ALU.mult))
            S.v(lambda: nc.vector.reduce_sum(var, sq, axis=AX.X))
            S.s(lambda eb=eb: nc.scalar.activation(out=rstd, in_=var, func=AF.Sqrt, bias=eb, scale=1.0 / D))
            S.v(lambda: nc.vector.reciprocal(out=rstd, in_=rstd))
            S.v(lambda: nc.vector.tensor_scalar(out=xt, in0=xt, scalar1=rstd, scalar2=None, op0=ALU.mult))
        for i in range(NT):
            sl = slice(i * 128, (i + 1) * 128)
            S.dma(lambda sl=sl: nc.sync.dma_start(out=xt, in_=x_dram[sl, :]))
            one_ln(eps)
            if xn32_dram is not None:
                S.dma(lambda sl=sl: nc.sync.dma_start(out=xn32_dram[sl, :], in_=xt))
            if double_ln:
                one_ln(eps2)
            for c in range(8):
                S.t(lambda c=c: nc.tensor.transpose(self.psum[:, 0:128], xt[:, c * 128:(c + 1) * 128], self.ident))
                if gb is not None:
                    S.v(lambda c=c: nc.vector.tensor_scalar(
                        out=tt, in0=self.psum[:, 0:128], scalar1=gsb[:, c:c + 1],
                        scalar2=bsb[:, c:c + 1], op0=ALU.mult, op1=ALU.add))
                else:
                    S.v(lambda: nc.vector.tensor_copy(out=tt, in_=self.psum[:, 0:128]))
                for od in outT_drams:
                    S.dma(lambda od=od, c=c, sl=sl: nc.sync.dma_start(
                        out=od[c * 128:(c + 1) * 128, sl], in_=tt))

    def load_aT(self, aT_dram, K, Tn=T):
        nc, S = self.nc, self.S
        for kc in range(K // 128):
            S.dma(lambda kc=kc: nc.sync.dma_start(out=self.tmp2[:, 0:Tn], in_=aT_dram[kc * 128:(kc + 1) * 128, :]))
            S.v(lambda kc=kc: nc.vector.tensor_copy(out=self.pool_a[:, kc * Tn:(kc + 1) * Tn], in_=self.tmp2[:, 0:Tn]))

    def load_w(self, w_dram, K, N, pos=0):
        nc, S = self.nc, self.S
        for kc in range(K // 128):
            S.dma(lambda kc=kc: nc.sync.dma_start(out=self.tmp2[:, 0:N], in_=w_dram[kc * 128:(kc + 1) * 128, :]))
            S.v(lambda kc=kc: nc.vector.tensor_copy(out=self.pool_w[:, pos + kc * N:pos + (kc + 1) * N], in_=self.tmp2[:, 0:N]))

    def tproj(self, K, N, evac, Tn=T, wpos=0):
        nc, S = self.nc, self.S
        KC = K // 128
        for npo in range(N // 128):
            for f0 in range(0, Tn, 512):
                f1 = min(f0 + 512, Tn)
                for kc in range(KC):
                    S.t(lambda kc=kc, npo=npo, f0=f0, f1=f1: nc.tensor.matmul(
                        self.psum[:, 0:f1 - f0],
                        self.pool_w[:, wpos + kc * N + npo * 128: wpos + kc * N + (npo + 1) * 128],
                        self.pool_a[:, kc * Tn + f0: kc * Tn + f1],
                        start=(kc == 0), stop=(kc == KC - 1)))
                evac(npo, f0, f1)

    def nproj(self, K, N, evac, Tn=T, wpos=0):
        nc, S = self.nc, self.S
        KC = K // 128
        for i in range(Tn // 128):
            for nf in range(0, N, 512):
                nf1 = min(nf + 512, N)
                for kc in range(KC):
                    S.t(lambda kc=kc, i=i, nf=nf, nf1=nf1: nc.tensor.matmul(
                        self.psum[:, nf:nf1],
                        self.pool_a[:, kc * Tn + i * 128:kc * Tn + (i + 1) * 128],
                        self.pool_w[:, wpos + kc * N + nf: wpos + kc * N + nf1],
                        start=(kc == 0), stop=(kc == KC - 1)))
            evac(i)


def build_l0_attn():
    b = Blk("l0a", pool_a_len=8 * T, pool_w_len=8 * T)
    nc = b.nc
    wqT = b.win("wqT", [D, D]); wkT = b.win("wkT", [D, D]); wvT = b.win("wvT", [D, D])
    bq = b.win("bq", [D]); bk = b.win("bk", [D]); bv = b.win("bv", [D])
    xnT = b.scratch("xnT", [D, T])
    qT = b.scratch("qT", [D, T]); kT = b.scratch("kT", [D, T])
    vn = b.scratch("vn", [T, D])
    atd = b.scratch("atd", [T, T], BF16)
    oT = b.scratch("oT", [D, T])
    S = b.begin()
    scale = 1.0 / 32.0
    b.ln_pass(b.x, [xnT])
    b.load_aT(xnT, D)
    bsb = b.sb("l0a_b", [128, 24])
    b.load_bias_cols(bq, bsb, D)
    for (bd, off) in ((bk, 8), (bv, 16)):
        for c in range(8):
            S.dma(lambda bd=bd, off=off, c=c: nc.sync.dma_start(
                out=bsb[:, off + c:off + c + 1], in_=bd[c * 128:(c + 1) * 128].unsqueeze(1)))
    b.load_w(wqT, D, D)
    def evac_q(od, boff):
        def e(npo, f0, f1):
            S.v(lambda npo=npo, f0=f0, f1=f1: nc.vector.tensor_scalar(
                out=b.tmp1[:, 0:f1 - f0], in0=b.psum[:, 0:f1 - f0],
                scalar1=bsb[:, boff + npo:boff + npo + 1], scalar2=None, op0=ALU.add))
            S.dma(lambda npo=npo, f0=f0, f1=f1: nc.sync.dma_start(
                out=od[npo * 128:(npo + 1) * 128, f0:f1], in_=b.tmp1[:, 0:f1 - f0]))
        return e
    b.tproj(D, D, evac_q(qT, 0))
    b.load_w(wkT, D, D)
    b.tproj(D, D, evac_q(kT, 8))
    bvrow = b.sb("l0a_bvrow", [1, D])
    S.dma(lambda: nc.sync.dma_start(out=bvrow, in_=bv.unsqueeze(0)))
    bvB = b.sb("l0a_bvB", [128, D])
    b.bcast_row(bvrow, bvB, D)
    b.load_w(wvT, D, D)
    def evac_v(i):
        S.v(lambda: nc.vector.tensor_tensor(out=b.tmp1[:, 0:D], in0=b.psum[:, 0:D], in1=bvB, op=ALU.add))
        S.dma(lambda i=i: nc.sync.dma_start(out=vn[i * 128:(i + 1) * 128, :], in_=b.tmp1[:, 0:D]))
    b.nproj(D, D, evac_v)
    for (src, dst) in ((qT, b.pool_a), (kT, b.pool_w)):
        for kc in range(8):
            S.dma(lambda src=src, kc=kc: nc.sync.dma_start(out=b.tmp2, in_=src[kc * 128:(kc + 1) * 128, :]))
            S.v(lambda dst=dst, kc=kc: nc.vector.tensor_copy(out=dst[:, kc * T:(kc + 1) * T], in_=b.tmp2))
    qh, kh = b.pool_a, b.pool_w
    mcol = b.stat[:, 16:48]
    zcol = b.sb("l0a_zcol", [128, 32])
    nm = b.stat[:, 5:6]
    for qt in range(NT):
        for kf in range(8):
            for kc in range(8):
                S.t(lambda qt=qt, kf=kf, kc=kc: nc.tensor.matmul(
                    b.psum[:, kf * 512:(kf + 1) * 512],
                    qh[:, kc * T + qt * 128: kc * T + (qt + 1) * 128],
                    kh[:, kc * T + kf * 512: kc * T + (kf + 1) * 512],
                    start=(kc == 0), stop=(kc == 7)))
        S.v(lambda: nc.vector.tensor_copy(out=b.tmp1, in_=b.psum))
        S.v(lambda: nc.vector.reduce_max(nm, b.tmp1, axis=AX.X))
        S.s(lambda: nc.scalar.mul(nm, nm, -scale))
        S.s(lambda qt=qt: nc.scalar.activation(out=b.tmp1, in_=b.tmp1, func=AF.Exp,
                                               bias=nm, scale=scale, accum_out=zcol[:, qt:qt + 1]))
        S.v(lambda qt=qt: nc.vector.tensor_copy(out=mcol[:, qt:qt + 1], in_=nm))
    lnz = b.sb("l0a_lnz", [128, 32])
    S.s(lambda: nc.scalar.activation(out=lnz, in_=zcol, func=AF.Ln))
    S.v(lambda: nc.vector.tensor_tensor(out=lnz, in0=mcol, in1=lnz, op=ALU.subtract))
    rowd = b.scratch("rowd", [32, 128])
    S.t(lambda: nc.tensor.transpose(b.psum[0:32, 0:128], lnz, b.ident))
    S.v(lambda: nc.vector.tensor_copy(out=b.tmp1[0:32, 0:128], in_=b.psum[0:32, 0:128]))
    S.dma(lambda: nc.sync.dma_start(out=rowd, in_=b.tmp1[0:32, 0:128]))
    combrow = b.sb("l0a_comb", [1, T])
    S.dma(lambda: nc.sync.dma_start(out=combrow, in_=rowd.rearrange("a b -> (a b)").unsqueeze(0)))
    combB = b.tmp2
    b.bcast_row(combrow, combB, T)
    for kc in range(NT):
        for qf in range(8):
            for c in range(8):
                S.t(lambda kc=kc, qf=qf, c=c: nc.tensor.matmul(
                    b.psum[:, qf * 512:(qf + 1) * 512],
                    kh[:, c * T + kc * 128: c * T + (kc + 1) * 128],
                    qh[:, c * T + qf * 512: c * T + (qf + 1) * 512],
                    start=(c == 0), stop=(c == 7)))
        S.v(lambda: nc.vector.scalar_tensor_tensor(out=b.tmp1, in0=b.psum, scalar=scale,
                                                   in1=combB, op0=ALU.mult, op1=ALU.add))
        S.s(lambda: nc.scalar.activation(out=b.bfa, in_=b.tmp1, func=AF.Exp))
        S.dma(lambda kc=kc: nc.sync.dma_start(out=atd[kc * 128:(kc + 1) * 128, :], in_=b.bfa))
    for kc in range(NT):
        S.dma(lambda kc=kc: nc.sync.dma_start(out=b.tmp2[:, 0:D], in_=vn[kc * 128:(kc + 1) * 128, :]))
        S.v(lambda kc=kc: nc.vector.tensor_copy(out=kh[:, kc * D:(kc + 1) * D], in_=b.tmp2[:, 0:D]))
    for dhc in range(8):
        for kc in range(NT):
            S.dma(lambda kc=kc: nc.sync.dma_start(out=b.bfa, in_=atd[kc * 128:(kc + 1) * 128, :]))
            for qf in range(8):
                S.t(lambda dhc=dhc, kc=kc, qf=qf: nc.tensor.matmul(
                    b.psum[:, qf * 512:(qf + 1) * 512],
                    kh[:, kc * D + dhc * 128: kc * D + (dhc + 1) * 128],
                    b.bfa[:, qf * 512:(qf + 1) * 512],
                    start=(kc == 0), stop=(kc == NT - 1)))
        S.v(lambda: nc.vector.tensor_copy(out=b.tmp1, in_=b.psum))
        S.dma(lambda dhc=dhc: nc.sync.dma_start(out=oT[dhc * 128:(dhc + 1) * 128, :], in_=b.tmp1))
    xrt = b.sb("l0a_xrt", [128, 128])
    for c in range(8):
        S.dma(lambda c=c: nc.sync.dma_start(out=b.tmp2, in_=oT[c * 128:(c + 1) * 128, :]))
        for i in range(NT):
            S.t(lambda i=i: nc.tensor.transpose(b.psum[:, 0:128], b.tmp2[:, i * 128:(i + 1) * 128], b.ident))
            S.dma(lambda i=i, c=c: nc.sync.dma_start(out=xrt, in_=b.x[i * 128:(i + 1) * 128, c * 128:(c + 1) * 128]))
            S.v(lambda: nc.vector.tensor_tensor(out=b.tmp1[:, 0:128], in0=b.psum[:, 0:128], in1=xrt, op=ALU.add))
            S.dma(lambda i=i, c=c: nc.sync.dma_start(
                out=b.y[i * 128:(i + 1) * 128, c * 128:(c + 1) * 128], in_=b.tmp1[:, 0:128]))
    b.finish()

    def feed(core, x_np, ins, x_full=None):
        return dict(b.consts,
                    x_in=np_f32(x_np),
                    wqT=np_f32(ins["L0_attn_wq"].T), wkT=np_f32(ins["L0_attn_wk"].T),
                    wvT=np_f32(ins["L0_attn_wv"].T),
                    bq=np_f32(ins["L0_attn_bq"]), bk=np_f32(ins["L0_attn_bk"]),
                    bv=np_f32(ins["L0_attn_bv"]))
    return b, feed


def build_l0_ffn():
    b = Blk("l0f", pool_a_len=8 * 2048, pool_w_len=12288)
    nc = b.nc
    gwT = b.win("gwT", [D, 32]); gbias = b.win("gbias", [32])
    egT = b.win("egT", [32, D, 512]); euT = b.win("euT", [32, D, 512])
    edT = b.win("edT", [32, 512, D])
    sgT = b.win("sgT", [D, D]); suT = b.win("suT", [D, D]); sdT = b.win("sdT", [D, D])
    niota_d = b.cin("c_niota", np.tile(np.arange(32, 0, -1, dtype=np.float32), (128, 1)))
    xnT = b.scratch("xnT", [D, T])
    gshT = b.scratch("gshT", [D, T]); ushT = b.scratch("ushT", [D, T])
    mshT = b.scratch("mshT", [D, T])
    S = b.begin()
    b.ln_pass(b.x, [xnT])
    gw_sb = b.sb("l0f_gw", [128, 256])
    for kc in range(8):
        S.dma(lambda kc=kc: nc.sync.dma_start(out=gw_sb[:, kc * 32:(kc + 1) * 32], in_=gwT[kc * 128:(kc + 1) * 128, :]))
    gbrow = b.sb("l0f_gbrow", [1, 32])
    S.dma(lambda: nc.sync.dma_start(out=gbrow, in_=gbias.unsqueeze(0)))
    gbB = b.sb("l0f_gbB", [128, 32])
    b.bcast_row(gbrow, gbB, 32)
    niota = b.sb("l0f_niota", [128, 32])
    S.dma(lambda: nc.sync.dma_start(out=niota, in_=niota_d))
    cw = b.sb("l0f_cw", [128, 32 * 32])
    sc = b.sb("l0f_sc", [128, 32]); scc = b.sb("l0f_scc", [128, 32])
    msk = b.sb("l0f_msk", [128, 32]); tg = b.sb("l0f_tg", [128, 32])
    gs = b.sb("l0f_gs", [128, 4]); gm = b.sb("l0f_gm", [128, 4])
    st = b.stat
    for i in range(NT):
        for kc in range(8):
            S.dma(lambda i=i, kc=kc: nc.sync.dma_start(
                out=b.tmp2[:, kc * 128:(kc + 1) * 128],
                in_=xnT[kc * 128:(kc + 1) * 128, i * 128:(i + 1) * 128]))
        for kc in range(8):
            S.t(lambda kc=kc: nc.tensor.matmul(
                b.psum[:, 0:32], b.tmp2[:, kc * 128:(kc + 1) * 128],
                gw_sb[:, kc * 32:(kc + 1) * 32], start=(kc == 0), stop=(kc == 7)))
        S.s(lambda: nc.scalar.activation(out=sc, in_=b.psum[:, 0:32], func=AF.Sigmoid))
        S.v(lambda: nc.vector.tensor_tensor(out=scc, in0=sc, in1=gbB, op=ALU.add))
        for g in range(4):
            sl = scc[:, g * 8:(g + 1) * 8]
            S.v(lambda sl=sl: nc.vector.reduce_max(st[:, 0:1], sl, axis=AX.X))
            S.v(lambda sl=sl: nc.vector.tensor_scalar(out=tg[:, 0:8], in0=sl, scalar1=st[:, 0:1], scalar2=None, op0=ALU.is_equal))
            S.v(lambda sl=sl: nc.vector.scalar_tensor_tensor(out=tg[:, 8:16], in0=tg[:, 0:8], scalar=-1e30, in1=sl, op0=ALU.mult, op1=ALU.add))
            S.v(lambda: nc.vector.reduce_max(st[:, 1:2], tg[:, 8:16], axis=AX.X))
            S.v(lambda g=g: nc.vector.tensor_tensor(out=gs[:, g:g + 1], in0=st[:, 0:1], in1=st[:, 1:2], op=ALU.add))
        S.v(lambda: nc.vector.reduce_max(st[:, 0:1], gs, axis=AX.X))
        S.v(lambda: nc.vector.tensor_scalar(out=gm, in0=gs, scalar1=st[:, 0:1], scalar2=None, op0=ALU.is_equal))
        S.v(lambda: nc.vector.scalar_tensor_tensor(out=gm, in0=gm, scalar=-1e30, in1=gs, op0=ALU.mult, op1=ALU.add))
        S.v(lambda: nc.vector.reduce_max(st[:, 1:2], gm, axis=AX.X))
        S.v(lambda: nc.vector.tensor_scalar(out=gm, in0=gs, scalar1=st[:, 1:2], scalar2=None, op0=ALU.is_ge))
        S.v(lambda: nc.vector.tensor_copy(
            out=msk.rearrange("p (g e) -> p g e", e=8),
            in_=gm.unsqueeze(2).broadcast_to([128, 4, 8])))
        S.v(lambda: nc.vector.tensor_tensor(out=msk, in0=msk, in1=scc, op=ALU.mult))
        for r in range(8):
            S.v(lambda: nc.vector.reduce_max(st[:, 0:1], msk, axis=AX.X))
            S.v(lambda: nc.vector.tensor_scalar(out=tg, in0=msk, scalar1=st[:, 0:1], scalar2=None, op0=ALU.is_equal))
            S.v(lambda: nc.vector.tensor_tensor(out=tg, in0=tg, in1=niota, op=ALU.mult))
            S.v(lambda: nc.vector.reduce_max(st[:, 1:2], tg, axis=AX.X))
            S.v(lambda: nc.vector.tensor_scalar(out=tg, in0=tg, scalar1=st[:, 1:2], scalar2=None, op0=ALU.is_equal))
            S.v(lambda: nc.vector.scalar_tensor_tensor(out=msk, in0=tg, scalar=-1e30, in1=msk, op0=ALU.mult, op1=ALU.add))
        S.v(lambda: nc.vector.tensor_scalar(out=tg, in0=msk, scalar1=-1e29, scalar2=None, op0=ALU.is_lt))
        S.v(lambda: nc.vector.tensor_tensor(out=tg, in0=tg, in1=sc, op=ALU.mult))
        S.v(lambda: nc.vector.reduce_sum(st[:, 2:3], tg, axis=AX.X))
        S.v(lambda: nc.vector.tensor_scalar(out=st[:, 2:3], in0=st[:, 2:3], scalar1=1e-20, scalar2=None, op0=ALU.add))
        S.v(lambda: nc.vector.reciprocal(out=st[:, 2:3], in_=st[:, 2:3]))
        S.s(lambda: nc.scalar.mul(st[:, 2:3], st[:, 2:3], 2.5))
        S.v(lambda i=i: nc.vector.tensor_scalar(out=cw[:, i * 32:(i + 1) * 32], in0=tg, scalar1=st[:, 2:3], scalar2=None, op0=ALU.mult))
    b.load_aT(xnT, D, Tn=T) if False else None
    for h in range(2):
        t0 = h * 2048
        for kc in range(8):
            S.dma(lambda kc=kc, t0=t0: nc.sync.dma_start(out=b.tmp2[:, 0:2048], in_=xnT[kc * 128:(kc + 1) * 128, t0:t0 + 2048]))
            S.v(lambda kc=kc: nc.vector.tensor_copy(out=b.pool_a[:, kc * 2048:(kc + 1) * 2048], in_=b.tmp2[:, 0:2048]))
        for (wd, od, act) in ((sgT, gshT, AF.Silu), (suT, ushT, None)):
            b.load_w(wd, D, D)
            def evac_s(npo, f0, f1, od=od, act=act, t0=t0):
                if act is not None:
                    S.s(lambda f0=f0, f1=f1: nc.scalar.activation(out=b.tmp1[:, 0:f1 - f0], in_=b.psum[:, 0:f1 - f0], func=act))
                else:
                    S.v(lambda f0=f0, f1=f1: nc.vector.tensor_copy(out=b.tmp1[:, 0:f1 - f0], in_=b.psum[:, 0:f1 - f0]))
                S.dma(lambda npo=npo, f0=f0, f1=f1: nc.sync.dma_start(
                    out=od[npo * 128:(npo + 1) * 128, t0 + f0:t0 + f1], in_=b.tmp1[:, 0:f1 - f0]))
            b.tproj(D, D, evac_s, Tn=2048)
    for c in range(8):
        S.dma(lambda c=c: nc.sync.dma_start(out=b.tmp1, in_=gshT[c * 128:(c + 1) * 128, :]))
        S.dma(lambda c=c: nc.sync.dma_start(out=b.tmp2, in_=ushT[c * 128:(c + 1) * 128, :]))
        S.v(lambda: nc.vector.tensor_tensor(out=b.tmp1, in0=b.tmp1, in1=b.tmp2, op=ALU.mult))
        S.dma(lambda c=c: nc.sync.dma_start(out=mshT[c * 128:(c + 1) * 128, :], in_=b.tmp1))
    acc = b.sb("l0f_acc", [128, 8 * D])
    gbuf = b.sb("l0f_gbuf", [128, 4 * 1024])
    mbuf = b.sb("l0f_mbuf", [128, 4 * 1024], BF16)
    xt2 = b.sb("l0f_xt2", [128, D])
    for qq in range(4):
        t0 = qq * 1024
        S.v(lambda: nc.vector.memset(acc, 0.0))
        for kc in range(8):
            S.dma(lambda kc=kc, t0=t0: nc.sync.dma_start(out=b.tmp2[:, 0:1024], in_=xnT[kc * 128:(kc + 1) * 128, t0:t0 + 1024]))
            S.v(lambda kc=kc: nc.vector.tensor_copy(out=b.pool_a[:, kc * 1024:(kc + 1) * 1024], in_=b.tmp2[:, 0:1024]))
        for e in range(32):
            b.load_w(egT[e], D, 512, pos=0)
            b.load_w(euT[e], D, 512, pos=4096)
            for kc in range(4):
                S.dma(lambda e=e, kc=kc: nc.sync.dma_start(out=b.tmp2[:, 0:1024], in_=edT[e][kc * 128:(kc + 1) * 128, :]))
                S.v(lambda kc=kc: nc.vector.tensor_copy(out=b.pool_w[:, 8192 + kc * 1024:8192 + (kc + 1) * 1024], in_=b.tmp2[:, 0:1024]))
            for npo in range(4):
                for f0 in range(0, 1024, 512):
                    for kc in range(8):
                        S.t(lambda npo=npo, f0=f0, kc=kc: nc.tensor.matmul(
                            b.psum[:, 0:512],
                            b.pool_w[:, kc * 512 + npo * 128:kc * 512 + (npo + 1) * 128],
                            b.pool_a[:, kc * 1024 + f0:kc * 1024 + f0 + 512],
                            start=(kc == 0), stop=(kc == 7)))
                    S.s(lambda npo=npo, f0=f0: nc.scalar.activation(
                        out=gbuf[:, npo * 1024 + f0:npo * 1024 + f0 + 512], in_=b.psum[:, 0:512], func=AF.Silu))
                    for kc in range(8):
                        S.t(lambda npo=npo, f0=f0, kc=kc: nc.tensor.matmul(
                            b.psum[:, 0:512],
                            b.pool_w[:, 4096 + kc * 512 + npo * 128:4096 + kc * 512 + (npo + 1) * 128],
                            b.pool_a[:, kc * 1024 + f0:kc * 1024 + f0 + 512],
                            start=(kc == 0), stop=(kc == 7)))
                    S.v(lambda npo=npo, f0=f0: nc.vector.tensor_tensor(
                        out=mbuf[:, npo * 1024 + f0:npo * 1024 + f0 + 512],
                        in0=b.psum[:, 0:512],
                        in1=gbuf[:, npo * 1024 + f0:npo * 1024 + f0 + 512], op=ALU.mult))
            for it in range(8):
                gt = qq * 8 + it
                for nf in range(0, D, 512):
                    for kc in range(4):
                        S.t(lambda it=it, nf=nf, kc=kc: nc.tensor.matmul(
                            b.psum[:, nf:nf + 512],
                            mbuf[:, kc * 1024 + it * 128:kc * 1024 + (it + 1) * 128],
                            b.pool_w[:, 8192 + kc * 1024 + nf:8192 + kc * 1024 + nf + 512],
                            start=(kc == 0), stop=(kc == 3)))
                S.v(lambda it=it, gt=gt, e=e: nc.vector.scalar_tensor_tensor(
                    out=acc[:, it * D:(it + 1) * D], in0=b.psum[:, 0:D],
                    scalar=cw[:, gt * 32 + e:gt * 32 + e + 1],
                    in1=acc[:, it * D:(it + 1) * D], op0=ALU.mult, op1=ALU.add))
        for kc in range(8):
            S.dma(lambda kc=kc, t0=t0: nc.sync.dma_start(out=b.tmp2[:, 0:1024], in_=mshT[kc * 128:(kc + 1) * 128, t0:t0 + 1024]))
            S.v(lambda kc=kc: nc.vector.tensor_copy(out=b.pool_a[:, kc * 1024:(kc + 1) * 1024], in_=b.tmp2[:, 0:1024]))
        b.load_w(sdT, D, D)
        for it in range(8):
            gt = qq * 8 + it
            sl = slice(gt * 128, (gt + 1) * 128)
            for nf in range(0, D, 512):
                for kc in range(8):
                    S.t(lambda it=it, nf=nf, kc=kc: nc.tensor.matmul(
                        b.psum[:, nf:nf + 512],
                        b.pool_a[:, kc * 1024 + it * 128:kc * 1024 + (it + 1) * 128],
                        b.pool_w[:, kc * D + nf:kc * D + nf + 512],
                        start=(kc == 0), stop=(kc == 7)))
            S.dma(lambda sl=sl: nc.sync.dma_start(out=xt2, in_=b.x[sl, :]))
            S.v(lambda it=it: nc.vector.tensor_tensor(out=b.tmp1[:, 0:D], in0=b.psum[:, 0:D], in1=acc[:, it * D:(it + 1) * D], op=ALU.add))
            S.v(lambda: nc.vector.tensor_tensor(out=b.tmp1[:, 0:D], in0=b.tmp1[:, 0:D], in1=xt2, op=ALU.add))
            S.dma(lambda sl=sl: nc.sync.dma_start(out=b.y[sl, :], in_=b.tmp1[:, 0:D]))
    b.finish()

    def feed(core, x_np, ins, x_full=None):
        return dict(b.consts,
                    x_in=np_f32(x_np),
                    gwT=np_f32(ins["L0_ffn_gate_w"].T), gbias=np_f32(ins["L0_ffn_gate_bias"]),
                    egT=np_f32(np.transpose(ins["L0_ffn_expert_gate"], (0, 2, 1))),
                    euT=np_f32(np.transpose(ins["L0_ffn_expert_up"], (0, 2, 1))),
                    edT=np_f32(np.transpose(ins["L0_ffn_expert_down"], (0, 2, 1))),
                    sgT=np_f32(ins["L0_ffn_shared_gate"].T),
                    suT=np_f32(ins["L0_ffn_shared_up"].T),
                    sdT=np_f32(ins["L0_ffn_shared_down"].T))
    return b, feed


def build_l2_ffn():
    b = Blk("l2f", pool_a_len=8 * 1024, pool_w_len=8 * 1024, x_shape=(NCORES * T, D))
    nc = b.nc
    CAP = 5120
    Wr = b.win("Wr", [D, 8])
    W1 = b.win("W1", [8, D, 4096])
    W2 = b.win("W2", [8, 4096, D])
    xown = b.win("xown", [T, D])
    ltri_d = b.cin("c_ltri", np.tril(np.ones((128, 128), dtype=np.float32)).T)
    onescol_d = b.cin("c_onescol", np.ones((128, 1), dtype=np.float32))
    ownoh = b.win("ownoh", [1, 8])
    xnT = b.scratch("xnTo", [D, T])
    xn_own = b.scratch("xn_own", [T, D])
    S = b.begin()
    wr_sb = b.sb("l2f_wr", [128, 64])
    for kc in range(8):
        S.dma(lambda kc=kc: nc.sync.dma_start(out=wr_sb[:, kc * 8:(kc + 1) * 8], in_=Wr[kc * 128:(kc + 1) * 128, :]))
    ltri = b.sb("l2f_ltri", [128, 128])
    S.dma(lambda: nc.sync.dma_start(out=ltri, in_=ltri_d))
    onescol = b.sb("l2f_onescol", [128, 1])
    S.dma(lambda: nc.sync.dma_start(out=onescol, in_=onescol_d))
    base = b.sb("l2f_base", [1, 8])
    S.v(lambda: nc.vector.memset(base, 0.0))
    cw_all = b.sb("l2f_cwall", [128, NCORES * NT * 8])
    pw_all = b.sb("l2f_pwall", [128, NCORES * NT])
    st = b.stat
    xt = b.tmp1[:, 0:D]; sq = b.tmp1[:, D:2 * D]
    mu = st[:, 0:1]; var = st[:, 1:2]; rstd = st[:, 2:3]; epsb = st[:, 3:4]
    oh = b.sb("l2f_oh", [128, 8]); incl = b.sb("l2f_incl", [128, 8])
    pmax = st[:, 5:6]; kept = st[:, 6:7]; lmax = st[:, 7:8]
    S.v(lambda: nc.vector.memset(epsb, 1e-5))
    for i in range(NCORES * NT):
        sl = slice(i * 128, (i + 1) * 128)
        S.dma(lambda sl=sl: nc.sync.dma_start(out=xt, in_=b.x[sl, :]))
        S.v(lambda: nc.vector.reduce_sum(mu, xt, axis=AX.X))
        S.s(lambda: nc.scalar.mul(mu, mu, 1.0 / D))
        S.v(lambda: nc.vector.tensor_scalar(out=xt, in0=xt, scalar1=mu, scalar2=None, op0=ALU.subtract))
        S.v(lambda: nc.vector.tensor_tensor(out=sq, in0=xt, in1=xt, op=ALU.mult))
        S.v(lambda: nc.vector.reduce_sum(var, sq, axis=AX.X))
        S.s(lambda: nc.scalar.activation(out=rstd, in_=var, func=AF.Sqrt, bias=epsb, scale=1.0 / D))
        S.v(lambda: nc.vector.reciprocal(out=rstd, in_=rstd))
        S.v(lambda: nc.vector.tensor_scalar(out=xt, in0=xt, scalar1=rstd, scalar2=None, op0=ALU.mult))
        for c in range(8):
            S.t(lambda c=c: nc.tensor.transpose(b.psum[:, 512 + c * 128:512 + (c + 1) * 128], xt[:, c * 128:(c + 1) * 128], b.ident))
        S.v(lambda: nc.vector.tensor_copy(out=b.tmp2[:, 0:D], in_=b.psum[:, 512:512 + D]))
        for c in range(8):
            S.t(lambda c=c: nc.tensor.matmul(
                b.psum[:, 0:8], b.tmp2[:, c * 128:(c + 1) * 128], wr_sb[:, c * 8:(c + 1) * 8],
                start=(c == 0), stop=(c == 7)))
        S.v(lambda: nc.vector.reduce_max(lmax, b.psum[:, 0:8], axis=AX.X))
        S.v(lambda: nc.vector.tensor_scalar(out=oh, in0=b.psum[:, 0:8], scalar1=lmax, scalar2=None, op0=ALU.is_equal))
        S.s(lambda: nc.scalar.mul(lmax, lmax, -1.0))
        S.s(lambda: nc.scalar.activation(out=b.tmp2[:, 0:8], in_=b.psum[:, 0:8], func=AF.Exp, bias=lmax, scale=1.0, accum_out=pmax))
        S.v(lambda: nc.vector.reciprocal(out=pmax, in_=pmax))
        S.t(lambda: nc.tensor.matmul(b.psum[:, 128:136], ltri, oh, start=True, stop=False))
        S.t(lambda: nc.tensor.matmul(b.psum[:, 128:136], b.ones1[:, 0:128], base, start=False, stop=True))
        S.v(lambda: nc.vector.tensor_copy(out=incl, in_=b.psum[:, 128:136]))
        S.t(lambda: nc.tensor.matmul(b.psum[0:1, 136:144], onescol, oh, start=True, stop=True))
        S.v(lambda: nc.vector.tensor_tensor(out=oh, in0=oh, in1=incl, op=ALU.mult))
        S.v(lambda: nc.vector.reduce_sum(st[:, 8:9], oh, axis=AX.X))
        S.v(lambda: nc.vector.tensor_scalar(out=kept, in0=st[:, 8:9], scalar1=float(CAP) + 0.5, scalar2=None, op0=ALU.is_lt))
        S.v(lambda: nc.vector.tensor_scalar(out=oh, in0=oh, scalar1=0.5, scalar2=None, op0=ALU.is_ge))
        S.v(lambda: nc.vector.tensor_tensor(out=b.tmp2[:, 8:9], in0=kept, in1=pmax, op=ALU.mult))
        S.v(lambda i=i: nc.vector.tensor_scalar(out=oh, in0=oh, scalar1=b.tmp2[:, 8:9], scalar2=None, op0=ALU.mult))
        S.v(lambda i=i: nc.vector.tensor_copy(out=cw_all[:, i * 8:(i + 1) * 8], in_=oh))
        S.v(lambda: nc.vector.tensor_scalar(out=kept, in0=kept, scalar1=-1.0, scalar2=1.0, op0=ALU.mult, op1=ALU.add))
        S.v(lambda i=i: nc.vector.tensor_copy(out=pw_all[:, i:i + 1], in_=kept))
        S.v(lambda: nc.vector.tensor_scalar(out=b.tmp2[:, 16:24], in0=incl, scalar1=0.0, scalar2=None, op0=ALU.mult)) if False else None
        S.v(lambda: nc.vector.tensor_copy(out=b.tmp2[0:1, 32:40], in_=b.psum[0:1, 136:144]))
        S.v(lambda: nc.vector.tensor_tensor(out=base, in0=base, in1=b.tmp2[0:1, 32:40], op=ALU.add))
    xt2 = b.sb("l2f_xt2", [128, D])
    for i in range(NT):
        sl = slice(i * 128, (i + 1) * 128)
        S.dma(lambda sl=sl: nc.sync.dma_start(out=xt, in_=xown[sl, :]))
        S.v(lambda: nc.vector.reduce_sum(mu, xt, axis=AX.X))
        S.s(lambda: nc.scalar.mul(mu, mu, 1.0 / D))
        S.v(lambda: nc.vector.tensor_scalar(out=xt, in0=xt, scalar1=mu, scalar2=None, op0=ALU.subtract))
        S.v(lambda: nc.vector.tensor_tensor(out=sq, in0=xt, in1=xt, op=ALU.mult))
        S.v(lambda: nc.vector.reduce_sum(var, sq, axis=AX.X))
        S.s(lambda: nc.scalar.activation(out=rstd, in_=var, func=AF.Sqrt, bias=epsb, scale=1.0 / D))
        S.v(lambda: nc.vector.reciprocal(out=rstd, in_=rstd))
        S.v(lambda: nc.vector.tensor_scalar(out=xt, in0=xt, scalar1=rstd, scalar2=None, op0=ALU.mult))
        S.dma(lambda sl=sl: nc.sync.dma_start(out=xn_own[sl, :], in_=xt))
        for c in range(8):
            S.t(lambda c=c: nc.tensor.transpose(b.psum[:, 0:128], xt[:, c * 128:(c + 1) * 128], b.ident))
            S.v(lambda: nc.vector.tensor_copy(out=b.tmp2[:, 0:128], in_=b.psum[:, 0:128]))
            S.dma(lambda c=c, sl=sl: nc.sync.dma_start(out=xnT[c * 128:(c + 1) * 128, sl], in_=b.tmp2[:, 0:128]))
    cwown = b.sb("l2f_cwown", [128, 32 * 8])
    pwown = b.sb("l2f_pwown", [128, 32])
    ohrow = b.sb("l2f_ohrow", [1, 8])
    S.dma(lambda: nc.sync.dma_start(out=ohrow, in_=ownoh))
    ohB = b.sb("l2f_ohB", [128, 8])
    b.bcast_row(ohrow, ohB, 8)
    S.v(lambda: nc.vector.memset(cwown, 0.0))
    S.v(lambda: nc.vector.memset(pwown, 0.0))
    for i in range(NT):
        for c in range(NCORES):
            gt = c * NT + i
            S.v(lambda i=i, c=c, gt=gt: nc.vector.scalar_tensor_tensor(
                out=cwown[:, i * 8:(i + 1) * 8], in0=cw_all[:, gt * 8:(gt + 1) * 8],
                scalar=ohB[:, c:c + 1], in1=cwown[:, i * 8:(i + 1) * 8], op0=ALU.mult, op1=ALU.add))
            S.v(lambda i=i, c=c, gt=gt: nc.vector.scalar_tensor_tensor(
                out=pwown[:, i:i + 1], in0=pw_all[:, gt:gt + 1],
                scalar=ohB[:, c:c + 1], in1=pwown[:, i:i + 1], op0=ALU.mult, op1=ALU.add))
    acc = b.sb("l2f_acc", [128, 8 * D])
    hbuf = b.sb("l2f_hbuf", [128, 8 * 1024], BF16)
    for qq in range(4):
        t0 = qq * 1024
        S.v(lambda: nc.vector.memset(acc, 0.0))
        for kc in range(8):
            S.dma(lambda kc=kc, t0=t0: nc.sync.dma_start(out=b.tmp2[:, 0:1024], in_=xnT[kc * 128:(kc + 1) * 128, t0:t0 + 1024]))
            S.v(lambda kc=kc: nc.vector.tensor_copy(out=b.pool_a[:, kc * 1024:(kc + 1) * 1024], in_=b.tmp2[:, 0:1024]))
        for e in range(8):
            for ffg in range(4):
                b.load_w(W1[e][:, ffg * 1024:(ffg + 1) * 1024], D, 1024, pos=0)
                for npo in range(8):
                    for f0 in range(0, 1024, 512):
                        for kc in range(8):
                            S.t(lambda npo=npo, f0=f0, kc=kc: nc.tensor.matmul(
                                b.psum[:, 0:512],
                                b.pool_w[:, kc * 1024 + npo * 128:kc * 1024 + (npo + 1) * 128],
                                b.pool_a[:, kc * 1024 + f0:kc * 1024 + f0 + 512],
                                start=(kc == 0), stop=(kc == 7)))
                        S.v(lambda npo=npo, f0=f0: nc.vector.tensor_relu(
                            hbuf[:, npo * 1024 + f0:npo * 1024 + f0 + 512], b.psum[:, 0:512]))
                b.load_w(W2[e][ffg * 1024:(ffg + 1) * 1024, :], 1024, D, pos=0)
                for it in range(8):
                    gt = qq * 8 + it
                    for nf in range(0, D, 512):
                        for kc in range(8):
                            S.t(lambda it=it, nf=nf, kc=kc: nc.tensor.matmul(
                                b.psum[:, nf:nf + 512],
                                hbuf[:, kc * 1024 + it * 128:kc * 1024 + (it + 1) * 128],
                                b.pool_w[:, kc * D + nf:kc * D + nf + 512],
                                start=(kc == 0), stop=(kc == 7)))
                    S.v(lambda it=it, gt=gt, e=e: nc.vector.scalar_tensor_tensor(
                        out=acc[:, it * D:(it + 1) * D], in0=b.psum[:, 0:D],
                        scalar=cwown[:, gt * 8 + e:gt * 8 + e + 1],
                        in1=acc[:, it * D:(it + 1) * D], op0=ALU.mult, op1=ALU.add))
        for it in range(8):
            gt = qq * 8 + it
            sl = slice(gt * 128, (gt + 1) * 128)
            S.dma(lambda sl=sl: nc.sync.dma_start(out=xt2, in_=xown[sl, :]))
            S.dma(lambda sl=sl: nc.sync.dma_start(out=b.tmp1[:, 0:D], in_=xn_own[sl, :]))
            S.v(lambda it=it: nc.vector.tensor_tensor(out=xt2, in0=xt2, in1=acc[:, it * D:(it + 1) * D], op=ALU.add))
            S.v(lambda gt=gt: nc.vector.scalar_tensor_tensor(
                out=xt2, in0=b.tmp1[:, 0:D], scalar=pwown[:, gt:gt + 1], in1=xt2, op0=ALU.mult, op1=ALU.add))
            S.dma(lambda sl=sl: nc.sync.dma_start(out=b.y[sl, :], in_=xt2))
    b.finish()

    def feed(core, x_np, ins, x_full=None):
        oh = np.zeros((1, 8), np.float32); oh[0, core] = 1.0
        return dict(b.consts,
                    x_in=np_f32(x_full),
                    xown=np_f32(x_np),
                    ownoh=oh,
                    Wr=np_f32(ins["L2_ffn_Wr"]),
                    W1=np_f32(ins["L2_ffn_W1"]),
                    W2=np_f32(ins["L2_ffn_W2"]))
    return b, feed


def mha(b, qT_d, kT_d, v_d, v_is_normal, oT_d, heads, dh, scale, causal):
    nc, S = b.nc, b.S
    if not hasattr(b, "_mha_init"):
        b._mha_init = True
        b.qh = b.sb("mh_qh", [128, T], BF16)
        b.kh = b.sb("mh_kh", [128, T], BF16)
        b.vn = b.sb("mh_vn", [128, NT * 128], BF16)
        b.oacc = b.sb("mh_oacc", [128, T])
        b.zc = b.sb("mh_zc", [128, 32])
        b.rowd = b.scratch("mh_rowd", [32, 128])
        b.combrow = b.sb("mh_comb", [1, T])
        if causal:
            mq = np.zeros((128, 128), np.float32)
            for p in range(128):
                mq[p, p + 1:] = -1e30
            b.maskq_d = b.cin("c_maskq", mq)
            mt = np.zeros((4, 128, 512), np.float32)
            for r in range(4):
                for p in range(128):
                    for f0 in range(0, 512, 1):
                        pass
                    mt[r, p, 0:min(512, r * 128 + p)] = -1e30
            b.maskT_d = b.cin("c_maskT", mt.reshape(4 * 128, 512))
            b.maskq = b.sb("mh_mq", [128, 128])
            b.maskT = b.sb("mh_mt", [128, 4 * 512])
            S.dma(lambda: nc.sync.dma_start(out=b.maskq, in_=b.maskq_d))
            for r in range(4):
                S.dma(lambda r=r: nc.sync.dma_start(out=b.maskT[:, r * 512:(r + 1) * 512], in_=b.maskT_d[r * 128:(r + 1) * 128, :]))
    mcol = b.stat[:, 16:48]
    nm = b.stat[:, 5:6]
    for h in range(heads):
        r0 = h * dh
        for (src, dst) in ((qT_d, b.qh), (kT_d, b.kh)):
            S.dma(lambda src=src, r0=r0: nc.sync.dma_start(out=b.tmp1[0:dh, :], in_=src[r0:r0 + dh, :]))
            S.v(lambda dst=dst: nc.vector.tensor_copy(out=dst[0:dh, :], in_=b.tmp1[0:dh, :]))
        if v_is_normal:
            for kc in range(NT):
                S.dma(lambda kc=kc, r0=r0: nc.sync.dma_start(
                    out=b.tmp1[:, 0:dh], in_=v_d[kc * 128:(kc + 1) * 128, r0:r0 + dh]))
                S.v(lambda kc=kc: nc.vector.tensor_copy(out=b.vn[:, kc * dh:(kc + 1) * dh], in_=b.tmp1[:, 0:dh]))
        else:
            S.dma(lambda r0=r0: nc.sync.dma_start(out=b.tmp1[0:dh, :], in_=v_d[r0:r0 + dh, :]))
            for kc in range(NT):
                S.t(lambda kc=kc: nc.tensor.transpose(b.psum[:, 0:128], b.tmp1[0:dh, kc * 128:(kc + 1) * 128], b.ident))
                S.v(lambda kc=kc: nc.vector.tensor_copy(out=b.vn[:, kc * dh:(kc + 1) * dh], in_=b.psum[:, 0:dh]))
        for qt in range(NT):
            nkf = min(8, (qt * 128 + 128 + 511) // 512) if causal else 8
            for kf in range(nkf):
                S.t(lambda qt=qt, kf=kf: nc.tensor.matmul(
                    b.psum[:, kf * 512:(kf + 1) * 512],
                    b.qh[0:dh, qt * 128:(qt + 1) * 128],
                    b.kh[0:dh, kf * 512:(kf + 1) * 512],
                    start=True, stop=True))
            S.v(lambda nkf=nkf: nc.vector.tensor_copy(out=b.tmp1[:, 0:nkf * 512], in_=b.psum[:, 0:nkf * 512]))
            if causal:
                S.v(lambda qt=qt: nc.vector.tensor_tensor(
                    out=b.tmp1[:, qt * 128:(qt + 1) * 128],
                    in0=b.tmp1[:, qt * 128:(qt + 1) * 128], in1=b.maskq, op=ALU.add))
                if nkf * 512 > (qt + 1) * 128:
                    S.v(lambda qt=qt, nkf=nkf: nc.vector.memset(b.tmp1[:, (qt + 1) * 128:nkf * 512], -1e30))
            S.v(lambda nkf=nkf: nc.vector.reduce_max(nm, b.tmp1[:, 0:nkf * 512], axis=AX.X))
            S.s(lambda: nc.scalar.mul(nm, nm, -scale))
            S.s(lambda qt=qt, nkf=nkf: nc.scalar.activation(
                out=b.tmp1[:, 0:nkf * 512], in_=b.tmp1[:, 0:nkf * 512], func=AF.Exp,
                bias=nm, scale=scale, accum_out=b.zc[:, qt:qt + 1]))
            S.v(lambda qt=qt: nc.vector.tensor_copy(out=mcol[:, qt:qt + 1], in_=nm))
        lnz = b.stat[:, 48:64] if False else b.zc
        S.s(lambda: nc.scalar.activation(out=b.zc, in_=b.zc, func=AF.Ln))
        S.v(lambda: nc.vector.tensor_tensor(out=b.zc, in0=mcol, in1=b.zc, op=ALU.subtract))
        S.t(lambda: nc.tensor.transpose(b.psum[0:32, 0:128], b.zc, b.ident))
        S.v(lambda: nc.vector.tensor_copy(out=b.tmp1[0:32, 0:128], in_=b.psum[0:32, 0:128]))
        S.dma(lambda: nc.sync.dma_start(out=b.rowd, in_=b.tmp1[0:32, 0:128]))
        S.dma(lambda: nc.sync.dma_start(out=b.combrow, in_=b.rowd.rearrange("a b -> (a b)").unsqueeze(0)))
        combB = b.tmp2
        b.bcast_row(b.combrow, combB, T)
        S.v(lambda: nc.vector.memset(b.oacc[0:dh, :], 0.0))
        for kc in range(NT):
            j0 = kc // 4 if causal else 0
            for j in range(j0, 8):
                S.t(lambda kc=kc, j=j: nc.tensor.matmul(
                    b.psum[:, j * 512:(j + 1) * 512],
                    b.kh[0:dh, kc * 128:(kc + 1) * 128],
                    b.qh[0:dh, j * 512:(j + 1) * 512],
                    start=True, stop=True))
            S.v(lambda j0=j0: nc.vector.scalar_tensor_tensor(
                out=b.tmp1[:, j0 * 512:T], in0=b.psum[:, j0 * 512:T], scalar=scale,
                in1=combB[:, j0 * 512:T], op0=ALU.mult, op1=ALU.add))
            if causal:
                S.v(lambda kc=kc, j0=j0: nc.vector.tensor_tensor(
                    out=b.tmp1[:, j0 * 512:j0 * 512 + 512],
                    in0=b.tmp1[:, j0 * 512:j0 * 512 + 512],
                    in1=b.maskT[:, (kc % 4) * 512:(kc % 4 + 1) * 512], op=ALU.add))
            S.s(lambda j0=j0: nc.scalar.activation(out=b.bfa[:, j0 * 512:T], in_=b.tmp1[:, j0 * 512:T], func=AF.Exp))
            for j in range(j0, 8):
                S.t(lambda kc=kc, j=j: nc.tensor.matmul(
                    b.psum[0:dh, j * 512:(j + 1) * 512],
                    b.vn[:, kc * dh:(kc + 1) * dh],
                    b.bfa[:, j * 512:(j + 1) * 512],
                    start=True, stop=True))
            S.v(lambda j0=j0, dh=dh: nc.vector.tensor_tensor(
                out=b.oacc[0:dh, j0 * 512:T], in0=b.psum[0:dh, j0 * 512:T],
                in1=b.oacc[0:dh, j0 * 512:T], op=ALU.add))
        S.dma(lambda r0=r0, dh=dh: nc.sync.dma_start(out=oT_d[r0:r0 + dh, :], in_=b.oacc[0:dh, :]))


def build_l1_ffn():
    b = Blk("l1f", pool_a_len=8 * T, pool_w_len=8 * 1024)
    nc = b.nc
    ew = b.win("ew", [6, 3]); act_w = b.win("act_w", [6])
    We = b.win("We", [20, D, D]); be = b.win("be", [20, D])
    Wq = b.win("Wq", [6, D, D]); Wk = b.win("Wk", [6, D, D])
    Wv = b.win("Wv", [6, D, D]); Wo = b.win("Wo", [6, D, D])
    ln_g = b.win("ln_g", [D]); ln_b = b.win("ln_b", [D])
    xnT = b.scratch("xnT", [D, T])
    nodeT = [b.scratch("nodeT%d" % c, [D, T]) for c in range(6)]
    qTd = b.scratch("qTd", [D, T]); kTd = b.scratch("kTd", [D, T]); vTd = b.scratch("vTd", [D, T])
    qhT = b.scratch("qhT", [D, T]); khT = b.scratch("khT", [D, T]); vhT = b.scratch("vhT", [D, T])
    ocT = b.scratch("ocT", [D, T])
    S = b.begin()
    b.ln_pass(b.x, [xnT])
    ewrow = b.sb("l1f_ewrow", [1, 32])
    S.dma(lambda: nc.sync.dma_start(out=ewrow[:, 0:18], in_=ew.rearrange("a b -> (a b)").unsqueeze(0)))
    S.dma(lambda: nc.sync.dma_start(out=ewrow[:, 18:24], in_=act_w.unsqueeze(0)))
    ewB = b.sb("l1f_ewB", [128, 32])
    b.bcast_row(ewrow, ewB, 32)
    besb = b.sb("l1f_besb", [128, 8])
    def src(idx):
        return xnT if idx == -1 else nodeT[idx]
    lind = 0
    scale = 1.0 / np.sqrt(128.0)
    for c in range(6):
        nsrc = min(c + 1, 5)
        snode = c - nsrc
        eq, ek, ev = lind + nsrc - 1, lind, lind + nsrc // 2
        in_q, in_k, in_v = c - 1, snode, snode + nsrc // 2
        lind += nsrc
        for (j, eidx, sidx, od) in ((0, eq, in_q, qTd), (1, ek, in_k, kTd), (2, ev, in_v, vTd)):
            b.load_aT(src(sidx), D)
            b.load_w(We[eidx], D, D)
            b.load_bias_cols(be[eidx], besb, D)
            def evac_e(npo, f0, f1, od=od, j=j, c=c):
                S.v(lambda npo=npo, f0=f0, f1=f1: nc.vector.tensor_scalar(
                    out=b.tmp1[:, 0:f1 - f0], in0=b.psum[:, 0:f1 - f0],
                    scalar1=besb[:, npo:npo + 1], scalar2=ewB[:, c * 3 + j:c * 3 + j + 1],
                    op0=ALU.add, op1=ALU.mult))
                S.dma(lambda npo=npo, f0=f0, f1=f1: nc.sync.dma_start(
                    out=od[npo * 128:(npo + 1) * 128, f0:f1], in_=b.tmp1[:, 0:f1 - f0]))
            b.tproj(D, D, evac_e)
        for (ind, wd, od) in ((qTd, Wq[c], qhT), (kTd, Wk[c], khT), (vTd, Wv[c], vhT)):
            b.load_aT(ind, D)
            b.load_w(wd, D, D)
            def evac_p(npo, f0, f1, od=od):
                S.v(lambda f0=f0, f1=f1: nc.vector.tensor_copy(out=b.tmp1[:, 0:f1 - f0], in_=b.psum[:, 0:f1 - f0]))
                S.dma(lambda npo=npo, f0=f0, f1=f1: nc.sync.dma_start(
                    out=od[npo * 128:(npo + 1) * 128, f0:f1], in_=b.tmp1[:, 0:f1 - f0]))
            b.tproj(D, D, evac_p)
        mha(b, qhT, khT, vhT, False, ocT, 8, 128, scale, False)
        b.load_aT(ocT, D)
        b.load_w(Wo[c], D, D)
        qtile = b.sb("l1f_qt" + b.uid(), [128, 512])
        def evac_n(npo, f0, f1, c=c, qtile=qtile):
            S.dma(lambda npo=npo, f0=f0, f1=f1: nc.sync.dma_start(
                out=qtile[:, 0:f1 - f0], in_=qTd[npo * 128:(npo + 1) * 128, f0:f1]))
            S.v(lambda npo=npo, f0=f0, f1=f1: nc.vector.scalar_tensor_tensor(
                out=b.tmp1[:, 0:f1 - f0], in0=b.psum[:, 0:f1 - f0],
                scalar=ewB[:, 18 + c:19 + c], in1=qtile[:, 0:f1 - f0], op0=ALU.mult, op1=ALU.add))
            S.dma(lambda npo=npo, f0=f0, f1=f1, c=c: nc.sync.dma_start(
                out=nodeT[c][npo * 128:(npo + 1) * 128, f0:f1], in_=b.tmp1[:, 0:f1 - f0]))
        b.tproj(D, D, evac_n)
    grow = b.sb("l1f_grow", [1, D]); brow = b.sb("l1f_brow", [1, D])
    S.dma(lambda: nc.sync.dma_start(out=grow, in_=ln_g.unsqueeze(0)))
    S.dma(lambda: nc.sync.dma_start(out=brow, in_=ln_b.unsqueeze(0)))
    bfa32 = b.bfa.bitcast(F32)
    gB = bfa32[:, 0:D]; bB = bfa32[:, D:2 * D]
    b.bcast_row(grow, gB, D)
    b.bcast_row(brow, bB, D)
    st = b.stat
    stg = b.sb("l1f_stg", [128, 128])
    xt2 = b.sb("l1f_xt2", [128, D])
    S.v(lambda: nc.vector.memset(st[:, 3:4], 1e-6))
    for i in range(NT):
        sl = slice(i * 128, (i + 1) * 128)
        for c in range(8):
            S.dma(lambda c=c, sl=sl: nc.sync.dma_start(out=stg, in_=nodeT[5][c * 128:(c + 1) * 128, sl]))
            S.t(lambda: nc.tensor.transpose(b.psum[:, 0:128], stg, b.ident))
            S.v(lambda c=c: nc.vector.tensor_copy(out=b.tmp1[:, c * 128:(c + 1) * 128], in_=b.psum[:, 0:128]))
        xt = b.tmp1[:, 0:D]; sq = b.tmp1[:, D:2 * D]
        S.v(lambda: nc.vector.reduce_sum(st[:, 0:1], xt, axis=AX.X))
        S.s(lambda: nc.scalar.mul(st[:, 0:1], st[:, 0:1], 1.0 / D))
        S.v(lambda: nc.vector.tensor_scalar(out=xt, in0=xt, scalar1=st[:, 0:1], scalar2=None, op0=ALU.subtract))
        S.v(lambda: nc.vector.tensor_tensor(out=sq, in0=xt, in1=xt, op=ALU.mult))
        S.v(lambda: nc.vector.reduce_sum(st[:, 1:2], sq, axis=AX.X))
        S.s(lambda: nc.scalar.activation(out=st[:, 2:3], in_=st[:, 1:2], func=AF.Sqrt, bias=st[:, 3:4], scale=1.0 / D))
        S.v(lambda: nc.vector.reciprocal(out=st[:, 2:3], in_=st[:, 2:3]))
        S.v(lambda: nc.vector.tensor_scalar(out=xt, in0=xt, scalar1=st[:, 2:3], scalar2=None, op0=ALU.mult))
        S.v(lambda: nc.vector.tensor_tensor(out=xt, in0=xt, in1=gB, op=ALU.mult))
        S.v(lambda: nc.vector.tensor_tensor(out=xt, in0=xt, in1=bB, op=ALU.add))
        S.dma(lambda sl=sl: nc.sync.dma_start(out=xt2, in_=b.x[sl, :]))
        S.v(lambda: nc.vector.tensor_tensor(out=xt, in0=xt, in1=xt2, op=ALU.add))
        S.dma(lambda sl=sl: nc.sync.dma_start(out=b.y[sl, :], in_=xt))
    b.finish()

    def feed(core, x_np, ins, x_full=None):
        return dict(b.consts,
                    x_in=np_f32(x_np),
                    ew=np_f32(ins["L1_ffn_ew"]), act_w=np_f32(ins["L1_ffn_act_w"]),
                    We=np_f32(ins["L1_ffn_We"]), be=np_f32(ins["L1_ffn_be"]),
                    Wq=np_f32(ins["L1_ffn_Wq"]), Wk=np_f32(ins["L1_ffn_Wk"]),
                    Wv=np_f32(ins["L1_ffn_Wv"]), Wo=np_f32(ins["L1_ffn_Wo"]),
                    ln_g=np_f32(ins["L1_ffn_ln_g"]), ln_b=np_f32(ins["L1_ffn_ln_b"]))
    return b, feed


def build_l3_attn():
    b = Blk("l3a", pool_a_len=8 * 2048, pool_w_len=8 * 3072)
    nc = b.nc
    ln1g = b.win("ln1g", [D]); ln1b = b.win("ln1b", [D])
    ln2g = b.win("ln2g", [D]); ln2b = b.win("ln2b", [D])
    Wqkv = b.win("Wqkv", [D, 3 * D]); Wo = b.win("Wo", [D, D])
    W1 = b.win("W1", [D, 4 * D]); b1 = b.win("b1", [4 * D])
    W2 = b.win("W2", [4 * D, D]); b2 = b.win("b2", [D])
    invf = 10000.0 ** (-(np.arange(0, 64, 2, dtype=np.float64) / 64))
    th = np.arange(T, dtype=np.float64)[:, None] * invf[None, :]
    cosh = np.cos(th); sinh = np.sin(th)
    cos_full = np.tile(cosh, (1, 16)).astype(np.float32)
    sin_full = np.tile(sinh, (1, 16)).astype(np.float32)
    cos_d = b.cin("c_cos", cos_full); sin_d = b.cin("c_sin", sin_full)
    xmod = b.scratch("xmod", [T, D])
    h1T = b.scratch("h1T", [D, T])
    qrT = b.scratch("qrT", [D, T]); krT = b.scratch("krT", [D, T])
    vnd = b.scratch("vnd", [T, D])
    oTd = b.scratch("oTd", [D, T])
    x2d = b.scratch("x2d", [T, D])
    h2T = b.scratch("h2T", [D, T])
    ffT = b.scratch("ffT", [4 * D, T])
    S = b.begin()
    b.ln_pass(b.x, [h1T], xn32_dram=xmod, double_ln=True, gb=(ln1g, ln1b))
    cs = b.sb("l3a_cs", [128, 512]); sn = b.sb("l3a_sn", [128, 512])
    rot = b.sb("l3a_rot", [128, 1024])
    t1 = b.sb("l3a_t1", [128, 512]); t2 = b.sb("l3a_t2", [128, 512])
    for half in range(2):
        t0 = half * 2048
        for kc in range(8):
            S.dma(lambda kc=kc, t0=t0: nc.sync.dma_start(out=b.tmp2[:, 0:2048], in_=h1T[kc * 128:(kc + 1) * 128, t0:t0 + 2048]))
            S.v(lambda kc=kc: nc.vector.tensor_copy(out=b.pool_a[:, kc * 2048:(kc + 1) * 2048], in_=b.tmp2[:, 0:2048]))
        b.load_w(Wqkv, D, 3 * D)
        for it in range(16):
            gt = half * 16 + it
            sl = slice(gt * 128, (gt + 1) * 128)
            for nf in range(0, 3 * D, 512):
                for kc in range(8):
                    S.t(lambda it=it, nf=nf, kc=kc: nc.tensor.matmul(
                        b.psum[:, nf:nf + 512],
                        b.pool_a[:, kc * 2048 + it * 128:kc * 2048 + (it + 1) * 128],
                        b.pool_w[:, kc * 3 * D + nf:kc * 3 * D + nf + 512],
                        start=(kc == 0), stop=(kc == 7)))
            S.dma(lambda sl=sl: nc.sync.dma_start(out=cs, in_=cos_d[sl, :]))
            S.dma(lambda sl=sl: nc.sync.dma_start(out=sn, in_=sin_d[sl, :]))
            for (off, od) in ((0, qrT), (D, krT)):
                qe = b.psum.rearrange("p (a two) -> p a two", two=2)[:, off // 2:(off + D) // 2, 0]
                qo = b.psum.rearrange("p (a two) -> p a two", two=2)[:, off // 2:(off + D) // 2, 1]
                re = rot.rearrange("p (a two) -> p a two", two=2)[:, :, 0]
                ro = rot.rearrange("p (a two) -> p a two", two=2)[:, :, 1]
                S.v(lambda qe=qe: nc.vector.tensor_tensor(out=t1, in0=qe, in1=cs, op=ALU.mult))
                S.v(lambda qo=qo: nc.vector.tensor_tensor(out=t2, in0=qo, in1=sn, op=ALU.mult))
                S.v(lambda re=re: nc.vector.tensor_tensor(out=re, in0=t1, in1=t2, op=ALU.subtract))
                S.v(lambda qe=qe: nc.vector.tensor_tensor(out=t1, in0=qe, in1=sn, op=ALU.mult))
                S.v(lambda qo=qo: nc.vector.tensor_tensor(out=t2, in0=qo, in1=cs, op=ALU.mult))
                S.v(lambda ro=ro: nc.vector.tensor_tensor(out=ro, in0=t1, in1=t2, op=ALU.add))
                for c in range(8):
                    S.t(lambda c=c: nc.tensor.transpose(b.psum[:, 3 * D:3 * D + 128], rot[:, c * 128:(c + 1) * 128], b.ident))
                    S.v(lambda c=c: nc.vector.tensor_copy(out=b.tmp1[:, c * 128:(c + 1) * 128], in_=b.psum[:, 3 * D:3 * D + 128]))
                for c in range(8):
                    S.dma(lambda od=od, c=c, sl=sl: nc.sync.dma_start(
                        out=od[c * 128:(c + 1) * 128, sl], in_=b.tmp1[:, c * 128:(c + 1) * 128]))
            S.v(lambda: nc.vector.tensor_copy(out=b.tmp1[:, 0:D], in_=b.psum[:, 2 * D:3 * D]))
            S.dma(lambda sl=sl: nc.sync.dma_start(out=vnd[sl, :], in_=b.tmp1[:, 0:D]))
    mha(b, qrT, krT, vnd, True, oTd, 16, 64, 0.125, True)
    g2sb = b.sb("l3a_g2", [128, 8]); b2sb = b.sb("l3a_b2", [128, 8])
    b.load_bias_cols(ln2g, g2sb, D)
    b.load_bias_cols(ln2b, b2sb, D)
    st = b.stat
    xt2 = b.sb("l3a_xt2", [128, D])
    for half in range(2):
        t0 = half * 2048
        for kc in range(8):
            S.dma(lambda kc=kc, t0=t0: nc.sync.dma_start(out=b.tmp2[:, 0:2048], in_=oTd[kc * 128:(kc + 1) * 128, t0:t0 + 2048]))
            S.v(lambda kc=kc: nc.vector.tensor_copy(out=b.pool_a[:, kc * 2048:(kc + 1) * 2048], in_=b.tmp2[:, 0:2048]))
        b.load_w(Wo, D, D)
        for it in range(16):
            gt = half * 16 + it
            sl = slice(gt * 128, (gt + 1) * 128)
            for nf in range(0, D, 512):
                for kc in range(8):
                    S.t(lambda it=it, nf=nf, kc=kc: nc.tensor.matmul(
                        b.psum[:, nf:nf + 512],
                        b.pool_a[:, kc * 2048 + it * 128:kc * 2048 + (it + 1) * 128],
                        b.pool_w[:, kc * D + nf:kc * D + nf + 512],
                        start=(kc == 0), stop=(kc == 7)))
            S.dma(lambda sl=sl: nc.sync.dma_start(out=xt2, in_=xmod[sl, :]))
            xt = b.tmp1[:, 0:D]; sq = b.tmp1[:, D:2 * D]
            S.v(lambda: nc.vector.tensor_tensor(out=xt, in0=b.psum[:, 0:D], in1=xt2, op=ALU.add))
            S.dma(lambda sl=sl: nc.sync.dma_start(out=x2d[sl, :], in_=xt))
            S.v(lambda: nc.vector.reduce_sum(st[:, 0:1], xt, axis=AX.X))
            S.s(lambda: nc.scalar.mul(st[:, 0:1], st[:, 0:1], 1.0 / D))
            S.v(lambda: nc.vector.tensor_scalar(out=xt, in0=xt, scalar1=st[:, 0:1], scalar2=None, op0=ALU.subtract))
            S.v(lambda: nc.vector.tensor_tensor(out=sq, in0=xt, in1=xt, op=ALU.mult))
            S.v(lambda: nc.vector.reduce_sum(st[:, 1:2], sq, axis=AX.X))
            S.v(lambda: nc.vector.memset(st[:, 3:4], 1e-5))
            S.s(lambda: nc.scalar.activation(out=st[:, 2:3], in_=st[:, 1:2], func=AF.Sqrt, bias=st[:, 3:4], scale=1.0 / D))
            S.v(lambda: nc.vector.reciprocal(out=st[:, 2:3], in_=st[:, 2:3]))
            S.v(lambda: nc.vector.tensor_scalar(out=xt, in0=xt, scalar1=st[:, 2:3], scalar2=None, op0=ALU.mult))
            for c in range(8):
                S.t(lambda c=c: nc.tensor.transpose(b.psum[:, 1024:1152], xt[:, c * 128:(c + 1) * 128], b.ident))
                S.v(lambda c=c: nc.vector.tensor_scalar(
                    out=b.tmp2[:, 0:128], in0=b.psum[:, 1024:1152], scalar1=g2sb[:, c:c + 1],
                    scalar2=b2sb[:, c:c + 1], op0=ALU.mult, op1=ALU.add))
                S.dma(lambda c=c, sl=sl: nc.sync.dma_start(out=h2T[c * 128:(c + 1) * 128, sl], in_=b.tmp2[:, 0:128]))
    b1sb = b.sb("l3a_b1", [128, 32])
    b.load_bias_cols(b1, b1sb, 4 * D)
    for half in range(2):
        t0 = half * 2048
        for kc in range(8):
            S.dma(lambda kc=kc, t0=t0: nc.sync.dma_start(out=b.tmp2[:, 0:2048], in_=h2T[kc * 128:(kc + 1) * 128, t0:t0 + 2048]))
            S.v(lambda kc=kc: nc.vector.tensor_copy(out=b.pool_a[:, kc * 2048:(kc + 1) * 2048], in_=b.tmp2[:, 0:2048]))
        for wh in range(2):
            b.load_w(W1[:, wh * 2048:(wh + 1) * 2048], D, 2048)
            for npo in range(16):
                gn = wh * 16 + npo
                for f0 in range(0, 2048, 512):
                    for kc in range(8):
                        S.t(lambda npo=npo, f0=f0, kc=kc: nc.tensor.matmul(
                            b.psum[:, 0:512],
                            b.pool_w[:, kc * 2048 + npo * 128:kc * 2048 + (npo + 1) * 128],
                            b.pool_a[:, kc * 2048 + f0:kc * 2048 + f0 + 512],
                            start=(kc == 0), stop=(kc == 7)))
                    S.s(lambda gn=gn: nc.scalar.activation(out=b.tmp1[:, 0:512], in_=b.psum[:, 0:512],
                                                           func=AF.Gelu, bias=b1sb[:, gn:gn + 1]))
                    S.dma(lambda gn=gn, t0=t0, f0=f0: nc.sync.dma_start(
                        out=ffT[gn * 128:(gn + 1) * 128, t0 + f0:t0 + f0 + 512], in_=b.tmp1[:, 0:512]))
    b2row = b.sb("l3a_b2row", [1, D])
    S.dma(lambda: nc.sync.dma_start(out=b2row, in_=b2.unsqueeze(0)))
    bfa32 = b.bfa.bitcast(F32)
    b2B = bfa32[:, 0:D]
    b.bcast_row(b2row, b2B, D)
    xo = bfa32[:, D:2 * D]
    for qq in range(4):
        t0 = qq * 1024
        for itg in range(2):
            for ffh in range(2):
                for kc in range(16):
                    kg = ffh * 16 + kc
                    S.dma(lambda kg=kg, t0=t0: nc.sync.dma_start(out=b.tmp2[:, 0:1024], in_=ffT[kg * 128:(kg + 1) * 128, t0:t0 + 1024]))
                    S.v(lambda kc=kc: nc.vector.tensor_copy(out=b.pool_w[:, kc * 1024:(kc + 1) * 1024], in_=b.tmp2[:, 0:1024]))
                    S.dma(lambda kg=kg: nc.sync.dma_start(out=b.tmp2[:, 0:1024], in_=W2[kg * 128:(kg + 1) * 128, :]))
                    S.v(lambda kc=kc: nc.vector.tensor_copy(out=b.pool_a[:, kc * 1024:(kc + 1) * 1024], in_=b.tmp2[:, 0:1024]))
                for it4 in range(4):
                    it = itg * 4 + it4
                    for nf in range(0, D, 512):
                        for kc in range(16):
                            S.t(lambda it=it, it4=it4, nf=nf, kc=kc, ffh=ffh: nc.tensor.matmul(
                                b.psum[:, it4 * 1024 + nf:it4 * 1024 + nf + 512],
                                b.pool_w[:, kc * 1024 + it * 128:kc * 1024 + (it + 1) * 128],
                                b.pool_a[:, kc * 1024 + nf:kc * 1024 + nf + 512],
                                start=(ffh == 0 and kc == 0), stop=(ffh == 1 and kc == 15)))
            for it4 in range(4):
                gt = qq * 8 + itg * 4 + it4
                sl = slice(gt * 128, (gt + 1) * 128)
                S.dma(lambda sl=sl: nc.sync.dma_start(out=xt2, in_=b.x[sl, :]))
                S.dma(lambda sl=sl: nc.sync.dma_start(out=xo, in_=x2d[sl, :]))
                S.v(lambda it4=it4: nc.vector.tensor_tensor(out=b.tmp1[:, 0:D], in0=b.psum[:, it4 * 1024:(it4 + 1) * 1024], in1=b2B, op=ALU.add))
                S.v(lambda: nc.vector.tensor_tensor(out=b.tmp1[:, 0:D], in0=b.tmp1[:, 0:D], in1=xt2, op=ALU.add))
                S.v(lambda: nc.vector.tensor_tensor(out=b.tmp1[:, 0:D], in0=b.tmp1[:, 0:D], in1=xo, op=ALU.add))
                S.dma(lambda sl=sl: nc.sync.dma_start(out=b.y[sl, :], in_=b.tmp1[:, 0:D]))
    b.finish()

    def feed(core, x_np, ins, x_full=None):
        return dict(b.consts,
                    x_in=np_f32(x_np),
                    ln1g=np_f32(ins["L3_attn_ln1_g"]), ln1b=np_f32(ins["L3_attn_ln1_b"]),
                    ln2g=np_f32(ins["L3_attn_ln2_g"]), ln2b=np_f32(ins["L3_attn_ln2_b"]),
                    Wqkv=np_f32(ins["L3_attn_Wqkv"]), Wo=np_f32(ins["L3_attn_Wo"]),
                    W1=np_f32(ins["L3_attn_W1"]), b1=np_f32(ins["L3_attn_b1"]),
                    W2=np_f32(ins["L3_attn_W2"]), b2=np_f32(ins["L3_attn_b2"]))
    return b, feed


def build_l1_attn():
    b = Blk("l1a", pool_a_len=8 * T, pool_w_len=8 * 1024)
    nc = b.nc
    Wq = b.win("Wq", [D, D]); Wk = b.win("Wk", [D, D])
    Wv = b.win("Wv", [D, D]); Wo = b.win("Wo", [D, D])
    ln_g = b.win("ln_g", [D]); ln_b = b.win("ln_b", [D])
    epsI_d = b.cin("c_epsI", (1e-4 * np.eye(64)).astype(np.float32))
    xnm = b.scratch("xnm", [T, D])
    xnT = b.scratch("xnT", [D, T])
    qTd = b.scratch("qTd", [D, T]); kTd = b.scratch("kTd", [D, T]); vTd = b.scratch("vTd", [D, T])
    Gb = b.scratch("Gb", [512, 64 * 64]); Ab = b.scratch("Ab", [512, 64 * 64])
    Wrb = b.scratch("Wrb", [512, 64 * 64])
    oTd = b.scratch("oTd", [D, T])
    S = b.begin()
    b.ln_pass(b.x, [xnT], xn32_dram=xnm, double_ln=True, gb=(ln_g, ln_b))
    b.load_aT(xnT, D)
    for (wd, od) in ((Wq, qTd), (Wk, kTd), (Wv, vTd)):
        b.load_w(wd, D, D)
        def evac_p(npo, f0, f1, od=od):
            S.v(lambda f0=f0, f1=f1: nc.vector.tensor_copy(out=b.tmp1[:, 0:f1 - f0], in_=b.psum[:, 0:f1 - f0]))
            S.dma(lambda npo=npo, f0=f0, f1=f1, od=od: nc.sync.dma_start(
                out=od[npo * 128:(npo + 1) * 128, f0:f1], in_=b.tmp1[:, 0:f1 - f0]))
        b.tproj(D, D, evac_p)
    epsI = b.sb("l1a_epsI", [64, 64])
    S.dma(lambda: nc.sync.dma_start(out=epsI, in_=epsI_d))
    ktmp = b.sb("l1a_ktmp", [128, 128]); vtmp = b.sb("l1a_vtmp", [128, 128])
    kn = b.sb("l1a_kn", [128, 64]); vn = b.sb("l1a_vn", [128, 64])
    g64 = b.sb("l1a_g64", [64, 64])
    S.v(lambda: nc.vector.memset(ktmp, 0.0))
    S.v(lambda: nc.vector.memset(vtmp, 0.0))
    for h in range(16):
        for j in range(32):
            prob = h * 32 + j
            S.dma(lambda h=h, j=j: nc.sync.dma_start(out=ktmp[0:64, :], in_=kTd[h * 64:(h + 1) * 64, j * 128:(j + 1) * 128]))
            S.dma(lambda h=h, j=j: nc.sync.dma_start(out=vtmp[0:64, :], in_=vTd[h * 64:(h + 1) * 64, j * 128:(j + 1) * 128]))
            S.t(lambda: nc.tensor.transpose(b.psum[:, 0:128], ktmp, b.ident))
            S.v(lambda: nc.vector.tensor_copy(out=kn, in_=b.psum[:, 0:64]))
            S.t(lambda: nc.tensor.transpose(b.psum[:, 128:256], vtmp, b.ident))
            S.v(lambda: nc.vector.tensor_copy(out=vn, in_=b.psum[:, 128:192]))
            S.t(lambda: nc.tensor.matmul(b.psum[0:64, 256:320], kn, kn, start=True, stop=True))
            S.v(lambda: nc.vector.tensor_tensor(out=g64, in0=b.psum[0:64, 256:320], in1=epsI, op=ALU.add))
            S.dma(lambda prob=prob: nc.sync.dma_start(out=Gb[prob].rearrange("(i j) -> i j", j=64), in_=g64))
            S.t(lambda: nc.tensor.matmul(b.psum[0:64, 320:384], kn, vn, start=True, stop=True))
            S.v(lambda: nc.vector.tensor_copy(out=g64, in_=b.psum[0:64, 320:384]))
            S.dma(lambda prob=prob: nc.sync.dma_start(out=Ab[prob].rearrange("(i j) -> i j", j=64), in_=g64))
    aug = b.sb("l1a_aug", [128, 8192])
    bigs = b.sb("l1a_bigs", [128, 8192])
    Wbuf = b.sb("l1a_wbuf", [128, 4096])
    pivs = b.sb("l1a_pivs", [128, 64])
    cof = b.sb("l1a_cof", [128, 64])
    dtmp = b.sb("l1a_dtmp", [128, 64])
    dt1 = b.stat[:, 8:9]
    Qb = b.tmp2[:, 0:512]
    Anew = b.tmp2[:, 512:1024]
    UtW = b.tmp2[:, 1024:1536]
    aug3 = aug.rearrange("p (i j) -> p i j", j=128)
    bigs3 = bigs.rearrange("p (i j) -> p i j", j=128)
    W3 = Wbuf.rearrange("p (i m) -> p i m", m=64)
    WmT = Wbuf.rearrange("p (i m) -> p m i", m=64)
    Q3 = Qb.rearrange("p (i c) -> p i c", c=8)
    A3 = Anew.rearrange("p (i c) -> p i c", c=8)
    U3 = UtW.rearrange("p (c m) -> p c m", m=64)
    for gg in range(4):
        S.dma(lambda gg=gg: nc.sync.dma_start(
            out=aug3[:, :, 0:64], in_=Gb[gg * 128:(gg + 1) * 128, :].rearrange("p (i j) -> p i j", j=64)))
        S.dma(lambda gg=gg: nc.sync.dma_start(
            out=aug3[:, :, 64:128], in_=Ab[gg * 128:(gg + 1) * 128, :].rearrange("p (i j) -> p i j", j=64)))
        for k in range(64):
            piv = aug[:, k * 128 + k:k * 128 + k + 1]
            S.v(lambda piv=piv, k=k: nc.vector.tensor_copy(out=pivs[:, k:k + 1], in_=piv))
            S.v(lambda k=k: nc.vector.reciprocal(out=b.stat[:, 9:10], in_=pivs[:, k:k + 1]))
            S.v(lambda k=k: nc.vector.tensor_scalar(out=cof, in0=aug3[:, :, k], scalar1=b.stat[:, 9:10], scalar2=None, op0=ALU.mult))
            S.v(lambda k=k: nc.vector.memset(cof[:, k:k + 1], 0.0))
            S.v(lambda k=k: nc.vector.tensor_tensor(
                out=bigs3, in0=cof.unsqueeze(2).broadcast_to([128, 64, 128]),
                in1=aug[:, k * 128:(k + 1) * 128].unsqueeze(1).broadcast_to([128, 64, 128]), op=ALU.mult))
            S.v(lambda: nc.vector.tensor_tensor(out=aug, in0=aug, in1=bigs, op=ALU.subtract))
        S.v(lambda: nc.vector.reciprocal(out=pivs, in_=pivs))
        S.v(lambda: nc.vector.tensor_tensor(
            out=W3, in0=aug3[:, :, 64:128], in1=pivs.unsqueeze(2).broadcast_to([128, 64, 64]), op=ALU.mult))
        S.v(lambda: nc.vector.tensor_copy(out=A3, in_=W3[:, :, 0:8]))
        for itp in range(2):
            if itp == 1:
                for c in range(8):
                    S.v(lambda c=c: nc.vector.tensor_tensor(
                        out=bigs3[:, :, 0:64], in0=W3,
                        in1=Q3[:, :, c].unsqueeze(1).broadcast_to([128, 64, 64]), op=ALU.mult))
                    S.v(lambda c=c: nc.vector.tensor_reduce(
                        out=A3[:, :, c], in_=bigs3[:, :, 0:64], axis=AX.X, op=ALU.add))
            for c in range(8):
                for jj in range(c):
                    S.v(lambda c=c, jj=jj: nc.vector.tensor_tensor(out=dtmp, in0=Q3[:, :, jj], in1=A3[:, :, c], op=ALU.mult))
                    S.v(lambda: nc.vector.reduce_sum(dt1, dtmp, axis=AX.X))
                    S.v(lambda jj=jj: nc.vector.tensor_scalar(out=dtmp, in0=Q3[:, :, jj], scalar1=dt1, scalar2=None, op0=ALU.mult))
                    S.v(lambda c=c: nc.vector.tensor_tensor(out=A3[:, :, c], in0=A3[:, :, c], in1=dtmp, op=ALU.subtract))
                S.v(lambda c=c: nc.vector.tensor_tensor(out=dtmp, in0=A3[:, :, c], in1=A3[:, :, c], op=ALU.mult))
                S.v(lambda: nc.vector.reduce_sum(dt1, dtmp, axis=AX.X))
                S.s(lambda: nc.scalar.sqrt(dt1, dt1))
                S.v(lambda: nc.vector.reciprocal(out=dt1, in_=dt1))
                S.v(lambda c=c: nc.vector.tensor_scalar(out=Q3[:, :, c], in0=A3[:, :, c], scalar1=dt1, scalar2=None, op0=ALU.mult))
        for c in range(8):
            S.v(lambda c=c: nc.vector.tensor_tensor(
                out=bigs3[:, :, 0:64], in0=WmT,
                in1=Q3[:, :, c].unsqueeze(1).broadcast_to([128, 64, 64]), op=ALU.mult))
            S.v(lambda c=c: nc.vector.tensor_reduce(
                out=U3[:, c, :], in_=bigs3[:, :, 0:64], axis=AX.X, op=ALU.add))
        for c in range(8):
            dst = aug3[:, :, 0:64] if c == 0 else bigs3[:, :, 0:64]
            S.v(lambda c=c, dst=dst: nc.vector.tensor_tensor(
                out=dst, in0=Q3[:, :, c].unsqueeze(2).broadcast_to([128, 64, 64]),
                in1=U3[:, c, :].unsqueeze(1).broadcast_to([128, 64, 64]), op=ALU.mult))
            if c > 0:
                S.v(lambda: nc.vector.tensor_tensor(out=aug3[:, :, 0:64], in0=aug3[:, :, 0:64], in1=bigs3[:, :, 0:64], op=ALU.add))
        S.dma(lambda gg=gg: nc.sync.dma_start(
            out=Wrb[gg * 128:(gg + 1) * 128, :].rearrange("p (i j) -> p i j", j=64), in_=aug3[:, :, 0:64]))
    Ssb = b.sb("l1a_S", [64, 64])
    wr64 = b.sb("l1a_wr64", [64, 64])
    qc = b.sb("l1a_qc", [64, 128])
    osl = b.sb("l1a_osl", [64, 128])
    for h in range(16):
        S.v(lambda: nc.vector.memset(Ssb, 0.0))
        for j in range(32):
            prob = h * 32 + j
            S.dma(lambda prob=prob: nc.sync.dma_start(out=wr64, in_=Wrb[prob].rearrange("(i j) -> i j", j=64)))
            S.v(lambda: nc.vector.scalar_tensor_tensor(out=Ssb, in0=Ssb, scalar=0.9, in1=wr64, op0=ALU.mult, op1=ALU.add))
            S.dma(lambda h=h, j=j: nc.sync.dma_start(out=qc, in_=qTd[h * 64:(h + 1) * 64, j * 128:(j + 1) * 128]))
            S.t(lambda: nc.tensor.matmul(b.psum[0:64, 0:128], Ssb, qc, start=True, stop=True))
            S.s(lambda: nc.scalar.mul(osl, b.psum[0:64, 0:128], 0.125))
            S.dma(lambda h=h, j=j: nc.sync.dma_start(out=oTd[h * 64:(h + 1) * 64, j * 128:(j + 1) * 128], in_=osl))
    b.load_aT(oTd, D)
    b.load_w(Wo, D, D)
    xt2 = b.tmp2[:, 1536:1536 + D]
    def evac_y(i):
        sl = slice(i * 128, (i + 1) * 128)
        S.dma(lambda sl=sl: nc.sync.dma_start(out=xt2, in_=b.x[sl, :]))
        S.v(lambda: nc.vector.tensor_tensor(out=b.tmp1[:, 0:D], in0=b.psum[:, 0:D], in1=xt2, op=ALU.add))
        S.dma(lambda sl=sl: nc.sync.dma_start(out=xt2, in_=xnm[sl, :]))
        S.v(lambda: nc.vector.tensor_tensor(out=b.tmp1[:, 0:D], in0=b.tmp1[:, 0:D], in1=xt2, op=ALU.add))
        S.dma(lambda sl=sl: nc.sync.dma_start(out=b.y[sl, :], in_=b.tmp1[:, 0:D]))
    b.nproj(D, D, evac_y)
    b.finish()

    def feed(core, x_np, ins, x_full=None):
        return dict(b.consts,
                    x_in=np_f32(x_np),
                    Wq=np_f32(ins["L1_attn_Wq"]), Wk=np_f32(ins["L1_attn_Wk"]),
                    Wv=np_f32(ins["L1_attn_Wv"]), Wo=np_f32(ins["L1_attn_Wo"]),
                    ln_g=np_f32(ins["L1_attn_ln_g"]), ln_b=np_f32(ins["L1_attn_ln_b"]))
    return b, feed


def build_l2_attn():
    b = Blk("l2a", pool_a_len=8 * T, pool_w_len=8 * 1024, bfa=False)
    nc = b.nc
    Wu = b.win("Wu", [D, D]); bu = b.win("bu", [D])
    Wv = b.win("Wv", [D, D]); bv = b.win("bv", [D])
    W1 = b.win("W1", [64, 64]); b1 = b.win("b1", [64])
    W2 = b.win("W2", [64, 64]); b2 = b.win("b2", [64])
    W3 = b.win("W3", [64, D]); b3 = b.win("b3", [D])
    Wo = b.win("Wo", [D, D]); bo = b.win("bo", [D])
    pos = np.arange(T, dtype=np.float64)[:, None]
    div = np.exp(np.arange(0, 64, 2, dtype=np.float64) * -(np.log(10000.0) / 64))
    pe = np.zeros((T, 64))
    pe[:, 0::2] = np.sin(pos * div); pe[:, 1::2] = np.cos(pos * div)
    PT_d = b.cin("c_PT", pe.T.astype(np.float32))
    dec = np.exp(-0.01 * np.arange(T, dtype=np.float64)).astype(np.float32)
    dec_d = b.cin("c_dec", dec.reshape(NT, 128).T.copy())
    n1g = np.arange(32); k1g = np.arange(64); n2g = np.arange(128); k2g = np.arange(128)
    ang = lambda a: a * (2 * np.pi)
    D1 = np.exp(-2j * np.pi * np.outer(n1g, k1g) / 64.0)
    D1re_d = b.cin("c_D1re", D1.real.astype(np.float32))
    D1im_d = b.cin("c_D1im", D1.imag.astype(np.float32))
    Tw = np.exp(-2j * np.pi * np.outer(k1g, n2g) / 8192.0)
    Twre_d = b.cin("c_Twre", Tw.real.astype(np.float32))
    Twim_d = b.cin("c_Twim", Tw.imag.astype(np.float32))
    D2 = np.exp(-2j * np.pi * np.outer(n2g, k2g) / 128.0)
    D2re_d = b.cin("c_D2re", D2.real.astype(np.float32))
    D2im_d = b.cin("c_D2im", D2.imag.astype(np.float32))
    nD2im_d = b.cin("c_nD2im", (-D2.imag).astype(np.float32))
    E2 = np.exp(2j * np.pi * np.outer(k2g, n2g) / 128.0)
    E2re_d = b.cin("c_E2re", E2.real.astype(np.float32))
    E2im_d = b.cin("c_E2im", E2.imag.astype(np.float32))
    nE2im_d = b.cin("c_nE2im", (-E2.imag).astype(np.float32))
    Ti = np.exp(2j * np.pi * np.outer(n2g, k1g) / 8192.0)
    Tire_d = b.cin("c_Tire", Ti.real.astype(np.float32))
    Tiim_d = b.cin("c_Tiim", Ti.imag.astype(np.float32))
    Fm = np.exp(2j * np.pi * np.outer(k1g, n1g) / 64.0) / 8192.0
    Fre_d = b.cin("c_Fre", Fm.real.astype(np.float32))
    nFim_d = b.cin("c_nFim", (-Fm.imag).astype(np.float32))
    xnT = b.scratch("xnT", [D, T])
    ud = b.scratch("ud", [T, D]); vd = b.scratch("vd", [T, D])
    hd = b.scratch("hd", [T, D])
    Bb = b.scratch("Bb", [2, 64, 128 * 32])
    Qb = b.scratch("Qb", [2, 128, 64 * 32])
    convd = b.scratch("convd", [T, D])
    S = b.begin()
    b.ln_pass(b.x, [xnT])
    b.load_aT(xnT, D)
    for (wd, bd, od) in ((Wu, bu, ud), (Wv, bv, vd)):
        b.load_w(wd, D, D)
        brow = b.sb("l2a_brow" + b.uid(), [1, D])
        S.dma(lambda bd=bd, brow=brow: nc.sync.dma_start(out=brow, in_=bd.unsqueeze(0)))
        bB = b.sb("l2a_bB" + b.uid(), [128, D])
        b.bcast_row(brow, bB, D)
        def evac_uv(i, od=od, bB=bB):
            S.v(lambda bB=bB: nc.vector.tensor_tensor(out=b.tmp1[:, 0:D], in0=b.psum[:, 0:D], in1=bB, op=ALU.add))
            S.dma(lambda i=i, od=od: nc.sync.dma_start(out=od[i * 128:(i + 1) * 128, :], in_=b.tmp1[:, 0:D]))
        b.nproj(D, D, evac_uv)
    cB = b.sb("l2a_cB", [64, 8192])
    cB2 = b.sb("l2a_cB2", [128, 4096])
    cA = cB2[0:32, 0:4096]
    PT = cB[0:64, 0:4096]
    S.dma(lambda: nc.sync.dma_start(out=PT, in_=PT_d))
    w64 = b.sb("l2a_w64", [64, 128])
    h1 = cB[0:64, 4096:8192]
    h2 = None
    bcol = b.sb("l2a_bcol", [64, 2])
    dec_sb = b.sb("l2a_dec", [128, 32])
    S.dma(lambda: nc.sync.dma_start(out=dec_sb, in_=dec_d))
    def gelu_tanh(buf, n):
        sc2 = b.tmp2[0:64, 0:n]
        S.v(lambda: nc.vector.tensor_tensor(out=sc2, in0=buf, in1=buf, op=ALU.mult))
        S.v(lambda: nc.vector.tensor_tensor(out=sc2, in0=sc2, in1=buf, op=ALU.mult))
        S.v(lambda: nc.vector.scalar_tensor_tensor(out=sc2, in0=sc2, scalar=0.044715, in1=buf, op0=ALU.mult, op1=ALU.add))
        S.s(lambda: nc.scalar.activation(out=sc2, in_=sc2, func=AF.Tanh, scale=0.7978845608028654))
        S.v(lambda: nc.vector.tensor_scalar(out=sc2, in0=sc2, scalar1=1.0, scalar2=None, op0=ALU.add))
        S.v(lambda: nc.vector.scalar_tensor_tensor(out=buf, in0=buf, scalar=0.5, in1=sc2, op0=ALU.mult, op1=ALU.mult))
    for (wd, bd, src, dst) in ((W1, b1, PT, h1), (W2, b2, h1, h1)):
        S.dma(lambda wd=wd: nc.sync.dma_start(out=w64[0:64, 0:64], in_=wd))
        S.dma(lambda bd=bd: nc.sync.dma_start(out=bcol[:, 0:1], in_=bd.unsqueeze(1)))
        for f0 in range(0, T, 512):
            S.t(lambda f0=f0, src=src: nc.tensor.matmul(
                b.psum[0:64, 0:512], w64[0:64, 0:64], src[0:64, f0:f0 + 512], start=True, stop=True))
            S.v(lambda f0=f0: nc.vector.tensor_scalar(out=b.tmp1[0:64, f0 - (f0 // 4096) * 4096:f0 - (f0 // 4096) * 4096 + 512] if False else b.tmp2[0:64, 0:512], in0=b.psum[0:64, 0:512], scalar1=bcol[:, 0:1], scalar2=None, op0=ALU.add))
            S.v(lambda f0=f0, dst=dst: nc.vector.tensor_copy(out=dst[0:64, f0:f0 + 512], in_=b.tmp2[0:64, 0:512]))
        gelu_tanh(dst[0:64, 0:T], T) if False else gelu_tanh(h1[0:64, 0:T], T)
    w3sb = b.sb("l2a_w3", [64, D])
    S.dma(lambda: nc.sync.dma_start(out=w3sb, in_=W3))
    b3row = b.sb("l2a_b3row", [1, D])
    S.dma(lambda: nc.sync.dma_start(out=b3row, in_=b3.unsqueeze(0)))
    b3B = b.sb("l2a_b3B", [128, D])
    b.bcast_row(b3row, b3B, D)
    for i in range(NT):
        for nf in range(0, D, 512):
            S.t(lambda i=i, nf=nf: nc.tensor.matmul(
                b.psum[:, nf:nf + 512], h1[0:64, i * 128:(i + 1) * 128], w3sb[0:64, nf:nf + 512],
                start=True, stop=True))
        S.v(lambda: nc.vector.tensor_tensor(out=b.tmp1[:, 0:D], in0=b.psum[:, 0:D], in1=b3B, op=ALU.add))
        S.v(lambda i=i: nc.vector.tensor_scalar(out=b.tmp1[:, 0:D], in0=b.tmp1[:, 0:D], scalar1=dec_sb[:, i:i + 1], scalar2=None, op0=ALU.mult))
        S.dma(lambda i=i: nc.sync.dma_start(out=hd[i * 128:(i + 1) * 128, :], in_=b.tmp1[:, 0:D]))
    d1re = b.sb("l2a_d1re", [32, 64]); d1im = b.sb("l2a_d1im", [32, 64])
    twre = b.sb("l2a_twre", [64, 128]); twim = b.sb("l2a_twim", [64, 128])
    d2re = b.sb("l2a_d2re", [128, 128]); d2im = b.sb("l2a_d2im", [128, 128]); nd2im = b.sb("l2a_nd2im", [128, 128])
    e2re = b.sb("l2a_e2re", [128, 128]); e2im = b.sb("l2a_e2im", [128, 128]); ne2im = b.sb("l2a_ne2im", [128, 128])
    tire = b.sb("l2a_tire", [128, 64]); tiim = b.sb("l2a_tiim", [128, 64])
    fre = b.sb("l2a_fre", [64, 32]); nfim = b.sb("l2a_nfim", [64, 32])
    for (sb_, d_) in ((d1re, D1re_d), (d1im, D1im_d), (twre, Twre_d), (twim, Twim_d),
                      (d2re, D2re_d), (d2im, D2im_d), (nd2im, nD2im_d),
                      (e2re, E2re_d), (e2im, E2im_d), (ne2im, nE2im_d),
                      (tire, Tire_d), (tiim, Tiim_d), (fre, Fre_d), (nfim, nFim_d)):
        S.dma(lambda sb_=sb_, d_=d_: nc.sync.dma_start(out=sb_, in_=d_))
    def fft_fwd(src_d, g, cdst):
        S.dma(lambda src_d=src_d, g=g: nc.sync.dma_start(
            out=cA.rearrange("p (b d) -> p b d", d=32),
            in_=src_d[:, g * 32:(g + 1) * 32].rearrange("(a b) d -> a b d", b=128)))
        for fc in range(8):
            S.t(lambda fc=fc: nc.tensor.matmul(b.psum[0:64, 0:512], d1re, cA[:, fc * 512:(fc + 1) * 512], start=True, stop=True))
            S.t(lambda fc=fc: nc.tensor.matmul(b.psum[0:64, 512:1024], d1im, cA[:, fc * 512:(fc + 1) * 512], start=True, stop=True))
            n2o = fc * 16
            twr = twre[:, n2o:n2o + 16].unsqueeze(2).broadcast_to([64, 16, 32])
            twi = twim[:, n2o:n2o + 16].unsqueeze(2).broadcast_to([64, 16, 32])
            t3 = b.tmp2[0:64, 0:512].rearrange("p (a b) -> p a b", b=32)
            t4 = b.tmp2[0:64, 512:1024].rearrange("p (a b) -> p a b", b=32)
            pr = b.psum[0:64, 0:512].rearrange("p (a b) -> p a b", b=32)
            pi = b.psum[0:64, 512:1024].rearrange("p (a b) -> p a b", b=32)
            S.v(lambda pr=pr, twr=twr, t3=t3: nc.vector.tensor_tensor(out=t3, in0=pr, in1=twr, op=ALU.mult))
            S.v(lambda pi=pi, twi=twi, t4=t4: nc.vector.tensor_tensor(out=t4, in0=pi, in1=twi, op=ALU.mult))
            S.v(lambda fc=fc, t3=t3, t4=t4: nc.vector.tensor_tensor(
                out=cB[:, fc * 512:(fc + 1) * 512], in0=b.tmp2[0:64, 0:512], in1=b.tmp2[0:64, 512:1024], op=ALU.subtract))
            S.v(lambda pr=pr, twi=twi, t3=t3: nc.vector.tensor_tensor(out=t3, in0=pr, in1=twi, op=ALU.mult))
            S.v(lambda pi=pi, twr=twr, t4=t4: nc.vector.tensor_tensor(out=t4, in0=pi, in1=twr, op=ALU.mult))
            S.v(lambda fc=fc: nc.vector.tensor_tensor(
                out=cB[:, 4096 + fc * 512:4096 + (fc + 1) * 512], in0=b.tmp2[0:64, 0:512], in1=b.tmp2[0:64, 512:1024], op=ALU.add))
        S.dma(lambda: nc.sync.dma_start(out=Bb[0], in_=cB[:, 0:4096]))
        S.dma(lambda: nc.sync.dma_start(out=Bb[1], in_=cB[:, 4096:8192]))
        for ri in range(2):
            S.dma(lambda ri=ri: nc.sync.dma_start(
                out=cB2[:, ri * 2048:(ri + 1) * 2048].rearrange("p (k d) -> p k d", d=32),
                in_=Bb[ri].rearrange("k (n d) -> n k d", d=32)))
        for fc in range(4):
            sl = slice(fc * 512, (fc + 1) * 512)
            S.t(lambda sl=sl: nc.tensor.matmul(b.psum[:, 0:512], d2re, cB2[:, sl], start=True, stop=False))
            S.t(lambda sl=sl: nc.tensor.matmul(b.psum[:, 0:512], nd2im, cB2[:, 2048 + sl.start:2048 + sl.stop], start=False, stop=True))
            S.t(lambda sl=sl: nc.tensor.matmul(b.psum[:, 512:1024], d2im, cB2[:, sl], start=True, stop=False))
            S.t(lambda sl=sl: nc.tensor.matmul(b.psum[:, 512:1024], d2re, cB2[:, 2048 + sl.start:2048 + sl.stop], start=False, stop=True))
            S.v(lambda fc=fc, cdst=cdst: nc.vector.tensor_copy(out=cdst[:, fc * 512:(fc + 1) * 512], in_=b.psum[:, 0:512]))
            S.v(lambda fc=fc, cdst=cdst: nc.vector.tensor_copy(out=cdst[:, 2048 + fc * 512:2048 + (fc + 1) * 512], in_=b.psum[:, 512:1024]))
    for g in range(32):
        fft_fwd(vd, g, b.tmp1)
        fft_fwd(hd, g, b.tmp2)
        S.v(lambda: nc.vector.tensor_tensor(out=cB2[:, 0:2048], in0=b.tmp1[:, 0:2048], in1=b.tmp2[:, 0:2048], op=ALU.mult))
        S.v(lambda: nc.vector.tensor_tensor(out=cB2[:, 2048:4096], in0=b.tmp1[:, 2048:4096], in1=b.tmp2[:, 2048:4096], op=ALU.mult))
        S.v(lambda: nc.vector.tensor_tensor(out=cB2[:, 0:2048], in0=cB2[:, 0:2048], in1=cB2[:, 2048:4096], op=ALU.subtract))
        S.v(lambda: nc.vector.tensor_tensor(out=b.tmp1[:, 0:2048], in0=b.tmp1[:, 0:2048], in1=b.tmp2[:, 2048:4096], op=ALU.mult))
        S.v(lambda: nc.vector.tensor_tensor(out=b.tmp1[:, 2048:4096], in0=b.tmp1[:, 2048:4096], in1=b.tmp2[:, 0:2048], op=ALU.mult))
        S.v(lambda: nc.vector.tensor_tensor(out=cB2[:, 2048:4096], in0=b.tmp1[:, 0:2048], in1=b.tmp1[:, 2048:4096], op=ALU.add))
        for fc in range(4):
            sl = slice(fc * 512, (fc + 1) * 512)
            S.t(lambda sl=sl: nc.tensor.matmul(b.psum[:, 0:512], e2re, cB2[:, sl], start=True, stop=False))
            S.t(lambda sl=sl: nc.tensor.matmul(b.psum[:, 0:512], ne2im, cB2[:, 2048 + sl.start:2048 + sl.stop], start=False, stop=True))
            S.t(lambda sl=sl: nc.tensor.matmul(b.psum[:, 512:1024], e2im, cB2[:, sl], start=True, stop=False))
            S.t(lambda sl=sl: nc.tensor.matmul(b.psum[:, 512:1024], e2re, cB2[:, 2048 + sl.start:2048 + sl.stop], start=False, stop=True))
            k1o = fc * 16
            tr = tire[:, k1o:k1o + 16].unsqueeze(2).broadcast_to([128, 16, 32])
            ti = tiim[:, k1o:k1o + 16].unsqueeze(2).broadcast_to([128, 16, 32])
            pr = b.psum[:, 0:512].rearrange("p (a b) -> p a b", b=32)
            pi = b.psum[:, 512:1024].rearrange("p (a b) -> p a b", b=32)
            t3 = b.tmp1[:, 0:512].rearrange("p (a b) -> p a b", b=32)
            t4 = b.tmp1[:, 512:1024].rearrange("p (a b) -> p a b", b=32)
            S.v(lambda pr=pr, tr=tr, t3=t3: nc.vector.tensor_tensor(out=t3, in0=pr, in1=tr, op=ALU.mult))
            S.v(lambda pi=pi, ti=ti, t4=t4: nc.vector.tensor_tensor(out=t4, in0=pi, in1=ti, op=ALU.mult))
            S.v(lambda fc=fc: nc.vector.tensor_tensor(out=cB[:, fc * 512:(fc + 1) * 512] if False else cB2[:, fc * 512:(fc + 1) * 512], in0=b.tmp1[:, 0:512], in1=b.tmp1[:, 512:1024], op=ALU.subtract))
            S.v(lambda pr=pr, ti=ti, t3=t3: nc.vector.tensor_tensor(out=t3, in0=pr, in1=ti, op=ALU.mult))
            S.v(lambda pi=pi, tr=tr, t4=t4: nc.vector.tensor_tensor(out=t4, in0=pi, in1=tr, op=ALU.mult))
            S.v(lambda fc=fc: nc.vector.tensor_tensor(out=b.tmp2[:, fc * 512:(fc + 1) * 512], in0=b.tmp1[:, 0:512], in1=b.tmp1[:, 512:1024], op=ALU.add))
        S.dma(lambda: nc.sync.dma_start(out=Qb[0], in_=cB2[:, 0:2048]))
        S.dma(lambda: nc.sync.dma_start(out=Qb[1], in_=b.tmp2[:, 0:2048]))
        for ri in range(2):
            S.dma(lambda ri=ri: nc.sync.dma_start(
                out=cB[:, ri * 4096:(ri + 1) * 4096].rearrange("p (n d) -> p n d", d=32),
                in_=Qb[ri].rearrange("n (k d) -> k n d", d=32)))
        for fc in range(8):
            sl = slice(fc * 512, (fc + 1) * 512)
            S.t(lambda sl=sl: nc.tensor.matmul(b.psum[0:32, 0:512], fre, cB[:, sl], start=True, stop=False))
            S.t(lambda sl=sl: nc.tensor.matmul(b.psum[0:32, 0:512], nfim, cB[:, 4096 + sl.start:4096 + sl.stop], start=False, stop=True))
            S.v(lambda fc=fc: nc.vector.tensor_copy(out=cA[:, fc * 512:(fc + 1) * 512], in_=b.psum[0:32, 0:512]))
        S.dma(lambda g=g: nc.sync.dma_start(
            out=convd[:, g * 32:(g + 1) * 32].rearrange("(a b) d -> a b d", b=128),
            in_=cA.rearrange("p (b d) -> p b d", d=32)))
    ut = b.tmp2[:, D:2 * D]
    for i in range(NT):
        sl = slice(i * 128, (i + 1) * 128)
        S.dma(lambda sl=sl: nc.sync.dma_start(out=b.tmp1[:, 0:D], in_=convd[sl, :]))
        S.dma(lambda sl=sl: nc.sync.dma_start(out=ut, in_=ud[sl, :]))
        S.v(lambda: nc.vector.tensor_tensor(out=b.tmp1[:, 0:D], in0=b.tmp1[:, 0:D], in1=ut, op=ALU.mult))
        for c in range(8):
            S.t(lambda c=c: nc.tensor.transpose(b.psum[:, 0:128], b.tmp1[:, c * 128:(c + 1) * 128], b.ident))
            S.v(lambda c=c, i=i: nc.vector.tensor_copy(out=b.pool_a[:, c * T + i * 128:c * T + (i + 1) * 128], in_=b.psum[:, 0:128]))
    b.load_w(Wo, D, D)
    borow = b.sb("l2a_borow", [1, D])
    S.dma(lambda: nc.sync.dma_start(out=borow, in_=bo.unsqueeze(0)))
    boB = b.tmp2[:, 0:D]
    b.bcast_row(borow, boB, D)
    def evac_o(i):
        sl = slice(i * 128, (i + 1) * 128)
        S.dma(lambda sl=sl: nc.sync.dma_start(out=ut, in_=b.x[sl, :]))
        S.v(lambda: nc.vector.tensor_tensor(out=b.tmp1[:, 0:D], in0=b.psum[:, 0:D], in1=boB, op=ALU.add))
        S.v(lambda: nc.vector.tensor_tensor(out=b.tmp1[:, 0:D], in0=b.tmp1[:, 0:D], in1=ut, op=ALU.add))
        S.dma(lambda sl=sl: nc.sync.dma_start(out=b.y[sl, :], in_=b.tmp1[:, 0:D]))
    b.nproj(D, D, evac_o)
    b.finish()

    def feed(core, x_np, ins, x_full=None):
        return dict(b.consts,
                    x_in=np_f32(x_np),
                    Wu=np_f32(ins["L2_attn_Wu"]), bu=np_f32(ins["L2_attn_bu"]),
                    Wv=np_f32(ins["L2_attn_Wv"]), bv=np_f32(ins["L2_attn_bv"]),
                    W1=np_f32(ins["L2_attn_W1"]), b1=np_f32(ins["L2_attn_b1"]),
                    W2=np_f32(ins["L2_attn_W2"]), b2=np_f32(ins["L2_attn_b2"]),
                    W3=np_f32(ins["L2_attn_W3"]), b3=np_f32(ins["L2_attn_b3"]),
                    Wo=np_f32(ins["L2_attn_Wo"]), bo=np_f32(ins["L2_attn_bo"]))
    return b, feed


def build_l3_ffn():
    b = Blk("l3f", pool_a_len=8 * T, pool_w_len=8 * 1024)
    nc = b.nc
    WrT = b.win("WrT", [D, D])
    WkT = b.win("WkT", [D, D])
    WvT = b.win("WvT", [D, D])
    mur = b.win("mur", [D])
    muk = b.win("muk", [D])
    xnT = b.scratch("xnT", [D, T])
    xrT = b.scratch("xrT", [D, T])
    xkT = b.scratch("xkT", [D, T])
    kaT = b.scratch("kaT", [D, T])
    rn = b.scratch("rn", [T, D])
    S = b.begin()
    b.ln_pass(b.x, [xnT])
    mur_sb = b.sb("murs", [128, 8]); muk_sb = b.sb("muks", [128, 8])
    b.load_bias_cols(mur, mur_sb, D)
    b.load_bias_cols(muk, muk_sb, D)
    omr = b.sb("omr", [128, 8]); omk = b.sb("omk", [128, 8])
    S.v(lambda: nc.vector.tensor_scalar(out=omr, in0=mur_sb, scalar1=-1.0, scalar2=1.0, op0=ALU.mult, op1=ALU.add))
    S.v(lambda: nc.vector.tensor_scalar(out=omk, in0=muk_sb, scalar1=-1.0, scalar2=1.0, op0=ALU.mult, op1=ALU.add))
    xc = b.tmp1; pv = b.tmp2
    xr_t = b.sb("l3f_xr", [128, T])
    for c in range(8):
        S.dma(lambda c=c: nc.sync.dma_start(out=xc, in_=xnT[c * 128:(c + 1) * 128, :]))
        S.v(lambda: nc.vector.memset(pv[:, 0:1], 0.0))
        S.v(lambda: nc.vector.tensor_copy(out=pv[:, 1:T], in_=xc[:, 0:T - 1]))
        S.v(lambda c=c: nc.vector.tensor_scalar(out=xr_t, in0=xc, scalar1=mur_sb[:, c:c + 1], scalar2=None, op0=ALU.mult))
        S.v(lambda c=c: nc.vector.scalar_tensor_tensor(out=xr_t, in0=pv, scalar=omr[:, c:c + 1], in1=xr_t, op0=ALU.mult, op1=ALU.add))
        S.dma(lambda c=c: nc.sync.dma_start(out=xrT[c * 128:(c + 1) * 128, :], in_=xr_t))
        S.v(lambda c=c: nc.vector.tensor_scalar(out=xr_t, in0=xc, scalar1=muk_sb[:, c:c + 1], scalar2=None, op0=ALU.mult))
        S.v(lambda c=c: nc.vector.scalar_tensor_tensor(out=xr_t, in0=pv, scalar=omk[:, c:c + 1], in1=xr_t, op0=ALU.mult, op1=ALU.add))
        S.dma(lambda c=c: nc.sync.dma_start(out=xkT[c * 128:(c + 1) * 128, :], in_=xr_t))
    b.load_aT(xkT, D)
    b.load_w(WkT, D, D)
    def evac_k(npo, f0, f1):
        S.v(lambda f0=f0, f1=f1: nc.vector.tensor_relu(b.tmp1[:, 0:f1 - f0], b.psum[:, 0:f1 - f0]))
        S.s(lambda f0=f0, f1=f1: nc.scalar.activation(out=b.tmp1[:, 0:f1 - f0], in_=b.tmp1[:, 0:f1 - f0], func=AF.Square))
        S.dma(lambda npo=npo, f0=f0, f1=f1: nc.sync.dma_start(
            out=kaT[npo * 128:(npo + 1) * 128, f0:f1], in_=b.tmp1[:, 0:f1 - f0]))
    b.tproj(D, D, evac_k)
    b.load_aT(xrT, D)
    b.load_w(WrT, D, D)
    def evac_r(i):
        S.s(lambda: nc.scalar.activation(out=b.tmp1[:, 0:D], in_=b.psum[:, 0:D], func=AF.Sigmoid))
        S.dma(lambda i=i: nc.sync.dma_start(out=rn[i * 128:(i + 1) * 128, :], in_=b.tmp1[:, 0:D]))
    b.nproj(D, D, evac_r)
    b.load_aT(kaT, D)
    b.load_w(WvT, D, D)
    rt = b.sb("l3f_rt", [128, D]); xt2 = b.sb("l3f_xt2", [128, D])
    def evac_y(i):
        S.dma(lambda i=i: nc.sync.dma_start(out=rt, in_=rn[i * 128:(i + 1) * 128, :]))
        S.dma(lambda i=i: nc.sync.dma_start(out=xt2, in_=b.x[i * 128:(i + 1) * 128, :]))
        S.v(lambda: nc.vector.tensor_tensor(out=b.tmp1[:, 0:D], in0=b.psum[:, 0:D], in1=rt, op=ALU.mult))
        S.v(lambda: nc.vector.tensor_tensor(out=b.tmp1[:, 0:D], in0=b.tmp1[:, 0:D], in1=xt2, op=ALU.add))
        S.dma(lambda i=i: nc.sync.dma_start(out=b.y[i * 128:(i + 1) * 128, :], in_=b.tmp1[:, 0:D]))
    b.nproj(D, D, evac_y)
    b.finish()

    def feed(core, x_np, ins, x_full=None):
        return dict(b.consts,
                    x_in=np_f32(x_np),
                    WrT=np_f32(ins["L3_ffn_W_r"].T),
                    WkT=np_f32(ins["L3_ffn_W_k"].T),
                    WvT=np_f32(ins["L3_ffn_W_v"].T),
                    mur=np_f32(ins["L3_ffn_mu_r"]),
                    muk=np_f32(ins["L3_ffn_mu_k"]))
    return b, feed


def np_f32(a):
    return np.ascontiguousarray(a, dtype=np.float32)


BLOCKS = []


def kernel(**inputs):
    x = np_f32(inputs["L0_attn_x"])
    builders = [
        build_l0_attn, build_l0_ffn, build_l1_attn, build_l1_ffn,
        build_l2_attn, build_l2_ffn, build_l3_attn, build_l3_ffn,
    ]
    for bf in builders:
        b, feed = bf()
        xf = x.reshape(-1, D)
        in_maps = [feed(c, x[c], inputs, x_full=xf) for c in range(NCORES)]
        res = run_bass_kernel_spmd(b.nc, in_maps, core_ids=list(range(NCORES)))
        x = np.stack([np_f32(r["y_out"]) for r in res.results], axis=0)
    return x
